# version 02
# baseline (speedup 1.0000x reference)
import numpy as np
import ml_dtypes
from contextlib import ExitStack

import concourse.bass as bass
import concourse.mybir as mybir
from concourse.bass_utils import run_bass_kernel_spmd

BF = ml_dtypes.bfloat16
F32 = mybir.dt.float32
BF16 = mybir.dt.bfloat16
I32 = mybir.dt.int32
AF = mybir.ActivationFunctionType
OP = mybir.AluOpType
AX = mybir.AxisListType.X

NCORE = 8
R = 1024
D = 4096
NDC = 32


class PB:
    ENGS = ("sync", "pe", "act", "dve", "pool")
    SEMS = ("ld", "sta", "std", "pe", "act", "dve", "pool")

    def __init__(self, nc, stack):
        self.nc = nc
        self.steps = {e: [] for e in self.ENGS}
        self.cnt = {s: 0 for s in self.SEMS}
        self.hw = {e: {} for e in self.ENGS}
        self.sems = {s: stack.enter_context(nc.semaphore(s)) for s in self.SEMS}

    SERIAL = {"act": "act", "dve": "dve", "pool": "pool"}

    def emit(self, eng, fn, waits=(), inc=None, amt=1):
        waits = list(waits)
        own = self.SERIAL.get(eng)
        if own is not None and fn is not None:
            waits.append((own, self.cnt[own]))
        w = []
        for (s, v) in waits:
            if v is None or v <= 0:
                continue
            if v > self.hw[eng].get(s, 0):
                self.hw[eng][s] = v
                w.append((s, v))
        self.steps[eng].append((w, fn, inc, amt))
        if inc is not None:
            self.cnt[inc] += amt
            return self.cnt[inc]
        return None

    def wait_only(self, eng, waits):
        self.emit(eng, None, waits)

    def dma(self, eng, out_ap, in_ap, waits=(), inc="ld"):
        def f(e, o=out_ap, i=in_ap):
            return e.dma_start(out=o, in_=i)
        return self.emit(eng, f, waits, inc, 16)

    def finalize(self):
        with self.nc.Block() as blk:
            meths = {"sync": blk.sync, "pe": blk.tensor, "act": blk.scalar,
                     "dve": blk.vector, "pool": blk.gpsimd}
            for ename in self.ENGS:
                def runner(eng, _steps=self.steps[ename]):
                    for w, fn, inc, amt in _steps:
                        for (s, v) in w:
                            eng.wait_ge(self.sems[s], v)
                        if fn is None:
                            continue
                        ins = fn(eng)
                        if inc is not None:
                            ins.then_inc(self.sems[inc], amt)
                meths[ename](runner)


def mm(p, out_ap, lhsT_ap, rhs_ap, start, stop, waits=(), inc=None):
    def f(e, o=out_ap, l=lhsT_ap, r=rhs_ap, s=start, t=stop):
        return e.matmul(o, l, r, start=s, stop=t)
    return p.emit("pe", f, waits, inc)


def wload(p, sb, w_ap, nrows_dc, col0, cols, wst, wbf, war_pe):
    rounds = (nrows_dc + 7) // 8
    prev_cast = None
    for r_ in range(rounds):
        ndc = min(8, nrows_dc - r_ * 8)
        c_ld = None
        for i in range(ndc):
            dc = r_ * 8 + i
            waits = []
            if r_ == 0 and i == 0:
                if war_pe:
                    waits.append(("pe", war_pe))
                waits.append(("pool", p.cnt["pool"]))
                waits.append(("dve", p.cnt["dve"]))
            if prev_cast is not None and i == 0:
                waits.append(("pool", prev_cast))
            c_ld = p.dma("sync", wst[:, i * cols:(i + 1) * cols],
                         w_ap[dc * 128:(dc + 1) * 128, col0:col0 + cols], waits)
        def cast(e, o=wbf[:, r_ * 8 * cols:(r_ * 8 + ndc) * cols],
                 i=wst[:, 0:ndc * cols]):
            return e.tensor_copy(o, i)
        prev_cast = p.emit("pool", cast, [("ld", c_ld)], "pool")
    return prev_cast


def bcast_row(p, G, dst, row_ap, onr, n, waits=()):
    c_mm = mm(p, G[:, 0:n], onr, row_ap, True, True, waits=list(waits), inc="pe")
    def f(e, o=dst, i=G[:, 0:n]):
        return e.activation(o, i, AF.Copy)
    return p.emit("act", f, [("pe", c_mm)], "act")


def act_evac(p, out_ap, psum_ap, waits, func=AF.Copy, bias=0.0, scale=1.0):
    if func == AF.Copy and not isinstance(bias, float):
        func = AF.Identity
    def f(e, o=out_ap, i=psum_ap, fu=func, b=bias, s=scale):
        return e.activation(o, i, fu, bias=b, scale=s)
    return p.emit("act", f, waits, "act")


def dve_op(p, fn, waits=(), inc="dve", eng="dve"):
    return p.emit(eng, fn, waits, inc)


def layernorm_phase(p, x_h, xt, sq, xn_tm, xn_fm, stt, psT, psT2, idb_sb,
                    qsum=None):
    c_prev_ts = None
    c_prev_tr_pe = None
    c_evac = None
    c_xn_list = []
    for rt in range(8):
        waits = [("dve", c_prev_ts)] if c_prev_ts else []
        c_x = p.dma("sync", xt[:, :], x_h[rt * 128:(rt + 1) * 128, :], waits)
        dve_op(p, lambda e, o=stt[:, 0:1], i=xt[:, :]: e.reduce_sum(o, i, axis=AX),
               [("ld", c_x)])
        c_sq = act_evac(p, sq[:, :], xt[:, :], [("ld", c_x)], func=AF.Square)
        dve_op(p, lambda e, o=stt[:, 1:2], i=sq[:, :]: e.reduce_sum(o, i, axis=AX),
               [("act", c_sq)])
        dve_op(p, lambda e, o=stt[:, 2:3], i=stt[:, 0:1]: e.tensor_scalar_mul(o, i, 1.0 / 4096.0))
        dve_op(p, lambda e, o=stt[:, 3:4], i=stt[:, 1:2]: e.tensor_scalar_mul(o, i, 1.0 / 4096.0))
        dve_op(p, lambda e, o=stt[:, 4:5], i=stt[:, 2:3]: e.tensor_mul(o, i, i))
        dve_op(p, lambda e, o=stt[:, 5:6], a=stt[:, 3:4], b=stt[:, 4:5]:
               e.tensor_sub(o, a, b))
        c_var = dve_op(p, lambda e, o=stt[:, 5:6], i=stt[:, 5:6]:
                       e.tensor_single_scalar(o, i, 1e-5, OP.add))
        c_std = act_evac(p, stt[:, 6:7], stt[:, 5:6], [("dve", c_var)], func=AF.Sqrt)
        dve_op(p, lambda e, o=stt[:, 7:8], i=stt[:, 6:7]: e.reciprocal(o, i),
               [("act", c_std)])
        waits = [("pe", c_prev_tr_pe)] if c_prev_tr_pe else []
        c_ts = dve_op(p, lambda e, o=xn_tm[:, :], i=xt[:, :], m=stt[:, 2:3],
                      r=stt[:, 7:8]: e.tensor_scalar(o, i, m, r, OP.subtract, OP.mult),
                      waits)
        c_xn_list.append(c_ts)
        c_prev_ts = c_ts
        if qsum is not None:
            mcol_sb, xsum_sb, psV = qsum
            c_qd_prev = None
            for ch in range(8):
                w = [("dve", c_ts)]
                if c_qd_prev:
                    w.append(("dve", c_qd_prev))
                c_qm = mm(p, psV[0:1, 0:512], mcol_sb[:, rt:rt + 1],
                          xn_tm[:, ch * 512:(ch + 1) * 512], True, True, w, inc="pe")
                if rt == 0:
                    c_qd_prev = dve_op(p, lambda e, o=xsum_sb[0:1, ch * 512:(ch + 1) * 512],
                                       i=psV[0:1, 0:512]: e.tensor_copy(o, i),
                                       [("pe", c_qm)])
                else:
                    c_qd_prev = dve_op(p, lambda e, o=xsum_sb[0:1, ch * 512:(ch + 1) * 512],
                                       i=psV[0:1, 0:512]: e.tensor_add(o, o, i),
                                       [("pe", c_qm)])
        for c in range(32):
            pst = psT if (c % 2 == 0) else psT2
            w = [("dve", c_ts)]
            if c_evac is not None:
                w.append(("act", c_evac if c < 2 else c_evac))
            c_tr = p.emit("pe", lambda e, o=pst[:, :], i=xn_tm[:, c * 128:(c + 1) * 128],
                          idn=idb_sb[:, :]: e.transpose(o, i, idn),
                          w + ([("act", c_evac)] if c_evac else []), "pe")
            c_prev_tr_pe = c_tr
            c_evac = p.emit("act", lambda e, o=xn_fm[:, c * 1024 + rt * 128:
                                                     c * 1024 + rt * 128 + 128],
                            i=pst[:, :]: e.copy(o, i), [("pe", c_tr)], "act")
    return c_evac, c_xn_list


def build_A():
    nc = bass.Bass("TRN2", target_bir_lowering=False)
    st = ExitStack()
    x_h = nc.dram_tensor("x", [R, D], F32, kind="ExternalInput").ap()
    wqkv_h = nc.dram_tensor("wqkv", [D, 3072], F32, kind="ExternalInput").ap()
    bqkv_h = nc.dram_tensor("bqkv", [3072], F32, kind="ExternalInput").ap()
    bwt_h = nc.dram_tensor("bwt", [D, D], F32, kind="ExternalInput").ap()
    bb_h = nc.dram_tensor("bb", [D], F32, kind="ExternalInput").ap()
    rwt_h = nc.dram_tensor("rwt", [D, 15], F32, kind="ExternalInput").ap()
    rb_h = nc.dram_tensor("rb", [15], F32, kind="ExternalInput").ap()
    at_h = nc.dram_tensor("at", [D, 128], F32, kind="ExternalInput").ap()
    bfl_h = nc.dram_tensor("bfl", [128, D], F32, kind="ExternalInput").ap()
    em_h = nc.dram_tensor("em", [15], I32, kind="ExternalInput").ap()
    idb_h = nc.dram_tensor("idb", [128, 128], BF16, kind="ExternalInput").ap()
    idf_h = nc.dram_tensor("idf", [128, 128], F32, kind="ExternalInput").ap()
    onr_h = nc.dram_tensor("onr", [1, 128], F32, kind="ExternalInput").ap()
    esel_h = nc.dram_tensor("esel", [8, 128], F32, kind="ExternalInput").ap()
    qT_h = nc.dram_tensor("qT", [1024, R], BF16, kind="ExternalOutput").ap()
    kT_h = nc.dram_tensor("kT", [1024, R], BF16, kind="ExternalOutput").ap()
    v_h = nc.dram_tensor("v", [R, 1024], BF16, kind="ExternalOutput").ap()
    ffn_h = nc.dram_tensor("ffn", [R, D], F32, kind="ExternalOutput").ap()

    xt = st.enter_context(nc.sbuf_tensor("xt", [128, D], F32))
    xn_tm = st.enter_context(nc.sbuf_tensor("xn_tm", [128, D], BF16))
    xn = st.enter_context(nc.sbuf_tensor("xn", [128, NDC * R], BF16))
    wst = st.enter_context(nc.sbuf_tensor("wst", [128, 8 * 512], F32))
    sq = wst
    wbf = st.enter_context(nc.sbuf_tensor("wbf", [128, 32 * 512], BF16))
    stt = st.enter_context(nc.sbuf_tensor("stt", [128, 8], F32))
    bq = st.enter_context(nc.sbuf_tensor("bq", [128, 16], F32))
    vbrow = st.enter_context(nc.sbuf_tensor("vbrow", [1, 1024], F32))
    vbb = st.enter_context(nc.sbuf_tensor("vbb", [128, 1024], F32))
    rbrow = st.enter_context(nc.sbuf_tensor("rbrow", [1, 15], F32))
    rbb = st.enter_context(nc.sbuf_tensor("rbb", [128, 15], F32))
    emi = st.enter_context(nc.sbuf_tensor("emi", [1, 15], I32))
    emf = st.enter_context(nc.sbuf_tensor("emf", [1, 15], F32))
    mapb = st.enter_context(nc.sbuf_tensor("mapb", [128, 15], F32))
    idb_sb = st.enter_context(nc.sbuf_tensor("idbs", [128, 128], BF16))
    idf_sb = st.enter_context(nc.sbuf_tensor("idfs", [128, 128], F32))
    onr_sb = st.enter_context(nc.sbuf_tensor("onrs", [1, 128], F32))
    esel_sb = st.enter_context(nc.sbuf_tensor("esels", [8, 128], F32))
    at_bf = st.enter_context(nc.sbuf_tensor("atbf", [128, D], BF16))
    bfl_bf = st.enter_context(nc.sbuf_tensor("bflbf", [128, D], BF16))
    rwt_bf = st.enter_context(nc.sbuf_tensor("rwtbf", [128, 32 * 15], BF16))
    lgb = st.enter_context(nc.sbuf_tensor("lgb", [128, 8 * 15], F32))
    prb = st.enter_context(nc.sbuf_tensor("prb", [128, 15], F32))
    tmp15 = st.enter_context(nc.sbuf_tensor("tmp15", [128, 15], F32))
    ranks = st.enter_context(nc.sbuf_tensor("ranks", [128, 15], F32))
    msk15 = st.enter_context(nc.sbuf_tensor("msk15", [128, 15], F32))
    wgt = st.enter_context(nc.sbuf_tensor("wgt", [128, 15], F32))
    me = st.enter_context(nc.sbuf_tensor("me", [128, 15], F32))
    wsb = st.enter_context(nc.sbuf_tensor("wsb", [128, 8], F32))
    wT = st.enter_context(nc.sbuf_tensor("wT", [8, 1024], F32))
    wex = st.enter_context(nc.sbuf_tensor("wex", [128, 1024], F32))
    h_sb = st.enter_context(nc.sbuf_tensor("hsb", [128, 1024], F32))
    hw = st.enter_context(nc.sbuf_tensor("hw", [128, 1024], BF16))
    stg = [st.enter_context(nc.sbuf_tensor(f"stg{i}", [128, 1024], BF16)) for i in range(2)]
    stgf = [st.enter_context(nc.sbuf_tensor(f"stgf{i}", [128, 512], F32)) for i in range(2)]
    stgv = [st.enter_context(nc.sbuf_tensor(f"stgv{i}", [128, 512], BF16)) for i in range(2)]
    bbrow = st.enter_context(nc.sbuf_tensor("bbrow", [1, 512], F32))
    bcb = st.enter_context(nc.sbuf_tensor("bcb", [128, 512], F32))

    G = [st.enter_context(nc.psum_tensor(f"G{i}", [128, 512], F32)) for i in range(4)]
    psT = st.enter_context(nc.psum_tensor("psT", [128, 128], BF16))
    psT2 = st.enter_context(nc.psum_tensor("psT2", [128, 128], BF16))
    psTf = st.enter_context(nc.psum_tensor("psTf", [128, 128], F32))

    p = PB(nc, st)

    p.dma("sync", idb_sb[:, :], idb_h[:, :])
    p.dma("sync", idf_sb[:, :], idf_h[:, :])
    p.dma("sync", onr_sb[:, :], onr_h[:, :])
    p.dma("sync", esel_sb[:, :], esel_h[:, :])
    p.dma("sync", emi[:, :], em_h[:])
    p.dma("sync", rbrow[:, :], rb_h[:])
    p.dma("sync", vbrow[:, :], bqkv_h[2048:3072])
    c_c = None
    for of in range(16):
        c_c = p.dma("sync", bq[:, of:of + 1], bqkv_h[of * 128:(of + 1) * 128])
    c_emf = dve_op(p, lambda e, o=emf[:, :], i=emi[:, :]: e.tensor_copy(o, i),
                   [("ld", c_c)])
    c_mapb = bcast_row(p, G[2], mapb[:, :], emf[:, :], onr_sb[:, :], 15,
                       [("dve", c_emf), ("ld", c_c)])
    c_rbb = bcast_row(p, G[2], rbb[:, :], rbrow[:, :], onr_sb[:, :], 15,
                      [("ld", c_c), ("act", c_mapb)])
    c_vbb0 = bcast_row(p, G[2], vbb[:, 0:512], vbrow[0:1, 0:512], onr_sb[:, :], 512,
                       [("act", c_rbb)])
    c_vbb = bcast_row(p, G[2], vbb[:, 512:1024], vbrow[0:1, 512:1024], onr_sb[:, :], 512,
                      [("act", c_vbb0)])

    c_xn_done, _ = layernorm_phase(p, x_h, xt, sq, xn_tm, xn_fm=xn, stt=stt,
                                   psT=psT, psT2=psT2, idb_sb=idb_sb)

    c_war_pe = None
    c_sta_prev = {}
    n_store = 0
    for og in range(4):
        c_cast = wload(p, None, wqkv_h, NDC, og * 512, 512, wst, wbf, c_war_pe)
        for of_l in range(4):
            of = og * 4 + of_l
            for rh in range(2):
                bank = G[rh]; bi = rh
                c_mm = None
                for dc in range(NDC):
                    w = []
                    if dc == 0:
                        w = [("pool", c_cast), ("act", c_xn_done)]
                        if bi in c_sta_prev:
                            w.append(("act", c_sta_prev[bi]))
                    c_mm = mm(p, bank[:, :], wbf[:, dc * 512 + of_l * 128:dc * 512 + of_l * 128 + 128],
                              xn[:, dc * 1024 + rh * 512:dc * 1024 + rh * 512 + 512],
                              dc == 0, dc == NDC - 1, w, inc="pe" if dc == NDC - 1 else None)
                s = stg[of % 2]
                w = [("pe", c_mm)]
                if n_store >= 2:
                    w.append(("sta", (n_store - 1) * 16))
                c_ev = act_evac(p, s[:, rh * 512:(rh + 1) * 512], bank[:, :], w,
                                bias=bq[:, of:of + 1])
                c_sta_prev[bi] = c_ev
            dst = qT_h if of < 8 else kT_h
            p.dma("act", dst[(of % 8) * 128:(of % 8) * 128 + 128, :], stg[of % 2][:, :],
                  [], inc="sta")
            n_store += 1
        c_war_pe = p.cnt["pe"]

    n_std = 0
    c_std_prev = {}
    for ogv in range(2):
        c_cast = wload(p, None, wqkv_h, NDC, 2048 + ogv * 512, 512, wst, wbf, c_war_pe)
        for rs in range(8):
            bank = G[rs % 2]; bi = rs % 2
            c_mm = None
            for dc in range(NDC):
                w = []
                if dc == 0:
                    w = [("pool", c_cast), ("act", c_vbb)]
                    if bi in c_sta_prev:
                        w.append(("act", c_sta_prev.pop(bi)))
                    if bi in c_std_prev:
                        w.append(("dve", c_std_prev[bi]))
                c_mm = mm(p, bank[:, :], xn[:, dc * 1024 + rs * 128:dc * 1024 + rs * 128 + 128],
                          wbf[:, dc * 512:(dc + 1) * 512],
                          dc == 0, dc == NDC - 1, w, inc="pe" if dc == NDC - 1 else None)
            sv = stgv[rs % 2]
            w = [("pe", c_mm)]
            if n_std >= 2:
                w.append(("std", (n_std - 1) * 16))
            c_ev = dve_op(p, lambda e, o=sv[:, :], a=bank[:, :], b=vbb[:, ogv * 512:(ogv + 1) * 512]:
                          e.tensor_add(o, a, b), w)
            c_std_prev[bi] = c_ev
            p.dma("act", v_h[rs * 128:(rs + 1) * 128, ogv * 512:(ogv + 1) * 512], sv[:, :],
                  [("dve", c_ev)], inc="std")
            n_std += 1
        c_war_pe = p.cnt["pe"]

    c_cast = wload(p, None, rwt_h, NDC, 0, 15, wst, wbf, c_war_pe)
    c_rw = dve_op(p, lambda e, o=rwt_bf[:, :], i=wbf[:, 0:480]: e.tensor_copy(o, i),
                  [("pool", c_cast)], eng="pool", inc="pool")
    c_cast = wload(p, None, at_h, NDC, 0, 128, wst, wbf, None)
    c_at = dve_op(p, lambda e, o=at_bf[:, :], i=wbf[:, 0:4096]: e.tensor_copy(o, i),
                  [("pool", c_cast)], eng="pool", inc="pool")
    c_ld_bfl = None
    for i in range(4):
        c_ld_bfl = p.dma("sync", wst[:, i * 1024:(i + 1) * 1024],
                         bfl_h[:, i * 1024:(i + 1) * 1024], [("pool", c_at)])
    c_bfl = dve_op(p, lambda e, o=bfl_bf[:, :], i=wst[:, 0:4096]: e.tensor_copy(o, i),
                   [("ld", c_ld_bfl)], eng="pool", inc="pool")

    c_lg = {}
    for rs in range(8):
        bank = G[2]
        c_mm = None
        for dc in range(NDC):
            w = []
            if dc == 0:
                w = [("pool", c_rw), ("act", c_xn_done)]
                if rs > 0:
                    w.append(("dve", c_lg[rs - 1]))
            c_mm = mm(p, bank[:, 0:15], xn[:, dc * 1024 + rs * 128:dc * 1024 + rs * 128 + 128],
                      rwt_bf[:, dc * 15:(dc + 1) * 15],
                      dc == 0, dc == NDC - 1, w, inc="pe" if dc == NDC - 1 else None)
        c_lg[rs] = dve_op(p, lambda e, o=lgb[:, rs * 15:(rs + 1) * 15], a=bank[:, 0:15],
                          b=rbb[:, :]: e.tensor_add(o, a, b), [("pe", c_mm)])

    c_h = None
    for rh in range(2):
        bank = G[3]
        c_mm = None
        for dc in range(NDC):
            w = [("pool", c_at)] if dc == 0 else []
            if dc == 0 and c_h is not None:
                w.append(("act", c_h))
            c_mm = mm(p, bank[:, :], at_bf[:, dc * 128:(dc + 1) * 128],
                      xn[:, dc * 1024 + rh * 512:dc * 1024 + rh * 512 + 512],
                      dc == 0, dc == NDC - 1, w, inc="pe" if dc == NDC - 1 else None)
        c_h = act_evac(p, h_sb[:, rh * 512:(rh + 1) * 512], bank[:, :], [("pe", c_mm)])

    c_wT = None
    c_tr_prev = None
    for rs in range(8):
        lg = lgb[:, rs * 15:(rs + 1) * 15]
        dve_op(p, lambda e, o=stt[:, 0:1], i=lg: e.reduce_max(o, i, axis=AX),
               [("dve", c_lg[7])] + ([("pe", c_tr_prev)] if c_tr_prev else []))
        c_nm = dve_op(p, lambda e, o=stt[:, 1:2], i=stt[:, 0:1]: e.tensor_scalar_mul(o, i, -1.0))
        c_eu = act_evac(p, prb[:, :], lg, [("dve", c_nm)], func=AF.Exp, bias=stt[:, 1:2])
        dve_op(p, lambda e, o=stt[:, 2:3], i=prb[:, :]: e.reduce_sum(o, i, axis=AX),
               [("act", c_eu)])
        dve_op(p, lambda e, o=stt[:, 3:4], i=stt[:, 2:3]: e.reciprocal(o, i))
        dve_op(p, lambda e, o=prb[:, :], i=prb[:, :], s=stt[:, 3:4]: e.tensor_scalar_mul(o, i, s))
        for n in range(15):
            dve_op(p, lambda e, o=tmp15[:, :], i=lg, s=lg[:, n:n + 1]:
                   e.tensor_single_scalar(o, i, s, OP.is_gt))
            dve_op(p, lambda e, o=ranks[:, n:n + 1], i=tmp15[:, :]: e.reduce_sum(o, i, axis=AX))
        dve_op(p, lambda e, o=msk15[:, :], i=ranks[:, :]:
               e.tensor_single_scalar(o, i, 7.5, OP.is_lt))
        dve_op(p, lambda e, o=wgt[:, :], a=prb[:, :], b=msk15[:, :]: e.tensor_mul(o, a, b))
        c_w = None
        for ex in range(8):
            dve_op(p, lambda e, o=me[:, :], i=mapb[:, :], v=float(ex):
                   e.tensor_single_scalar(o, i, v, OP.is_equal), [("act", c_mapb)])
            dve_op(p, lambda e, o=me[:, :], a=wgt[:, :], b=me[:, :]: e.tensor_mul(o, a, b))
            c_w = dve_op(p, lambda e, o=wsb[:, ex:ex + 1], i=me[:, :]: e.reduce_sum(o, i, axis=AX))
        w = [("dve", c_w)]
        if c_wT is not None:
            w.append(("act", c_wT))
        c_tr = p.emit("pe", lambda e, o=psTf[0:8, :], i=wsb[:, :], idn=idf_sb[:, :]:
                      e.transpose(o, i, idn), w, "pe")
        c_tr_prev = c_tr
        c_wT = p.emit("act", lambda e, o=wT[0:8, rs * 128:(rs + 1) * 128], i=psTf[0:8, :]:
                      e.copy(o, i), [("pe", c_tr)], "act")
    c_wex = None
    for hh in range(2):
        c_mm = mm(p, G[2][:, :], esel_sb[:, :], wT[0:8, hh * 512:(hh + 1) * 512],
                  True, True, [("act", c_wT)] + ([("act", c_wex)] if c_wex else []), inc="pe")
        c_wex = act_evac(p, wex[:, hh * 512:(hh + 1) * 512], G[2][:, :], [("pe", c_mm)])
    c_hw = dve_op(p, lambda e, o=hw[:, :], a=h_sb[:, :], b=wex[:, :]: e.tensor_mul(o, a, b),
                  [("act", c_wex), ("act", c_h)])

    c_war_pe = p.cnt["pe"]
    c_bcb = None
    c_std_prev = {}
    for oc in range(8):
        c_cast = wload(p, None, bwt_h, NDC, oc * 512, 512, wst, wbf, c_war_pe)
        c_bb = p.dma("sync", bbrow[0:1, :], bb_h[oc * 512:(oc + 1) * 512],
                     [("act", c_bcb)] if c_bcb else [])
        c_bcb = bcast_row(p, G[2], bcb[:, :], bbrow[0:1, :],
                          onr_sb[:, :], 512,
                          [("ld", c_bb), ("dve", p.cnt["dve"]), ("act", p.cnt["act"])])
        for rs in range(8):
            bank = G[rs % 2]; bi = rs % 2
            c_mm = None
            for dc in range(NDC):
                w = []
                if dc == 0:
                    w = [("pool", c_cast), ("dve", c_hw), ("pool", c_bfl)]
                    if bi in c_std_prev:
                        w.append(("dve", c_std_prev[bi]))
                c_mm = mm(p, bank[:, :], xn[:, dc * 1024 + rs * 128:dc * 1024 + rs * 128 + 128],
                          wbf[:, dc * 512:(dc + 1) * 512], dc == 0, False, w)
            c_mm = mm(p, bank[:, :], hw[:, rs * 128:(rs + 1) * 128],
                      bfl_bf[:, oc * 512:(oc + 1) * 512], False, True, [], inc="pe")
            sf = stgf[rs % 2]
            w = [("pe", c_mm), ("act", c_bcb)]
            if n_std >= 2:
                w.append(("std", (n_std - 1) * 16))
            c_ev = dve_op(p, lambda e, o=sf[:, :], a=bank[:, :], b=bcb[:, :]:
                          e.tensor_add(o, a, b), w)
            c_std_prev[bi] = c_ev
            p.dma("act", ffn_h[rs * 128:(rs + 1) * 128, oc * 512:(oc + 1) * 512], sf[:, :],
                  [("dve", c_ev)], inc="std")
            n_std += 1
        c_war_pe = p.cnt["pe"]

    p.wait_only("sync", [("sta", p.cnt["sta"]), ("std", p.cnt["std"])])
    p.finalize()
    st.close()
    return nc


def build_B():
    nc = bass.Bass("TRN2", target_bir_lowering=False)
    st = ExitStack()
    x_h = nc.dram_tensor("x", [R, D], F32, kind="ExternalInput").ap()
    qT_h = nc.dram_tensor("qT", [1024, R], BF16, kind="ExternalInput").ap()
    KT_h = nc.dram_tensor("KT", [1024, 4096], BF16, kind="ExternalInput").ap()
    V_h = nc.dram_tensor("V", [4096, 1024], BF16, kind="ExternalInput").ap()
    wo_h = nc.dram_tensor("wo", [1024, D], F32, kind="ExternalInput").ap()
    bo_h = nc.dram_tensor("bo", [D], F32, kind="ExternalInput").ap()
    ffn_h = nc.dram_tensor("ffn", [R, D], F32, kind="ExternalInput").ap()
    onr_h = nc.dram_tensor("onr", [1, 128], F32, kind="ExternalInput").ap()
    onb_h = nc.dram_tensor("onb", [128, 1], BF16, kind="ExternalInput").ap()
    x1_h = nc.dram_tensor("x1", [R, D], F32, kind="ExternalOutput").ap()

    onr_sb = st.enter_context(nc.sbuf_tensor("onrs", [1, 128], F32))
    onb_sb = st.enter_context(nc.sbuf_tensor("onbs", [128, 1], BF16))
    O_sb = st.enter_context(nc.sbuf_tensor("Osb", [128, 8 * 1024], BF16))
    smal = st.enter_context(nc.sbuf_tensor("smal", [1, 1024], F32))
    sc2 = st.enter_context(nc.sbuf_tensor("sc2", [1, 8], F32))
    negm = st.enter_context(nc.sbuf_tensor("negm", [128, 1], F32))
    recipb = st.enter_context(nc.sbuf_tensor("recipb", [128, 512], F32))
    rcp_row = st.enter_context(nc.sbuf_tensor("rcprow", [1, 512], F32))

    G = [st.enter_context(nc.psum_tensor(f"G{i}", [128, 512], F32)) for i in range(4)]
    psV = st.enter_context(nc.psum_tensor("psV", [128, 512], F32))

    p = PB(nc, st)
    p.dma("sync", onr_sb[:, :], onr_h[:, :])
    p.dma("sync", onb_sb[:, :], onb_h[:, :])

    attn_st = ExitStack()
    q_sb = attn_st.enter_context(nc.sbuf_tensor("qsb", [128, 8 * 1024], BF16))
    K_sb = attn_st.enter_context(nc.sbuf_tensor("Ksb", [128, 8 * 4096], BF16))
    PT = attn_st.enter_context(nc.sbuf_tensor("PT", [128, 32 * 512], BF16))
    Vb = [attn_st.enter_context(nc.sbuf_tensor(f"Vb{i}", [128, 32 * 128], BF16)) for i in range(2)]
    sqb = attn_st.enter_context(nc.sbuf_tensor("sqb", [128, 4096], BF16))

    c_q = None
    for dc in range(8):
        c_q = p.dma("sync", q_sb[:, dc * 1024:(dc + 1) * 1024], qT_h[dc * 128:(dc + 1) * 128, :])
    c_k = None
    for dc in range(8):
        c_k = p.dma("sync", K_sb[:, dc * 4096:(dc + 1) * 4096], KT_h[dc * 128:(dc + 1) * 128, :])

    c_norm_evac = None
    c_norm_mm = None
    for half in range(2):
        c_mm = None
        for dc in range(8):
            w = [("ld", c_q)]
            if c_norm_mm is not None:
                w.append(("pe", c_norm_mm))
            c_s = act_evac(p, sqb[:, dc * 512:(dc + 1) * 512],
                           q_sb[:, dc * 1024 + half * 512:dc * 1024 + half * 512 + 512],
                           w, func=AF.Square)
            w = [("act", c_s)]
            if dc == 0 and c_norm_evac is not None:
                w.append(("dve", c_norm_evac))
            c_mm = mm(p, psV[0:1, :], onb_sb[:, :], sqb[:, dc * 512:(dc + 1) * 512],
                      dc == 0, dc == 7, w, inc="pe" if dc == 7 else None)
        c_norm_mm = c_mm
        c_norm_evac = dve_op(p, lambda e, o=smal[0:1, half * 512:(half + 1) * 512],
                             i=psV[0:1, :]: e.tensor_copy(o, i), [("pe", c_mm)])
    dve_op(p, lambda e, o=sc2[0:1, 0:1], i=smal[0:1, :]: e.reduce_max(o, i, axis=AX),
           [("dve", c_norm_evac)])
    c_kn = None
    for ch in range(8):
        c_mm = None
        for dc in range(8):
            c_s = act_evac(p, sqb[:, dc * 512:(dc + 1) * 512],
                           K_sb[:, dc * 4096 + ch * 512:dc * 4096 + ch * 512 + 512],
                           [("ld", c_k), ("pe", c_norm_mm)], func=AF.Square)
            w = [("act", c_s)]
            if dc == 0:
                w.append(("dve", c_norm_evac if c_kn is None else c_kn))
            c_mm = mm(p, psV[0:1, :], onb_sb[:, :], sqb[:, dc * 512:(dc + 1) * 512],
                      dc == 0, dc == 7, w, inc="pe" if dc == 7 else None)
        c_norm_mm = c_mm
        c_kn = dve_op(p, lambda e, o=smal[0:1, 0:512], i=psV[0:1, :]: e.tensor_copy(o, i),
                      [("pe", c_mm)])
        if ch == 0:
            c_kn = dve_op(p, lambda e, o=sc2[0:1, 1:2], i=smal[0:1, 0:512]:
                          e.reduce_max(o, i, axis=AX))
        else:
            dve_op(p, lambda e, o=sc2[0:1, 2:3], i=smal[0:1, 0:512]: e.reduce_max(o, i, axis=AX))
            c_kn = dve_op(p, lambda e, o=sc2[0:1, 1:2], a=sc2[0:1, 1:2], b=sc2[0:1, 2:3]:
                          e.tensor_max(o, a, b))
    c_m2 = dve_op(p, lambda e, o=sc2[0:1, 3:4], a=sc2[0:1, 0:1], b=sc2[0:1, 1:2]:
                  e.tensor_mul(o, a, b), [("dve", c_kn)])
    c_sqr = act_evac(p, sc2[0:1, 4:5], sc2[0:1, 3:4], [("dve", c_m2)], func=AF.Sqrt)
    c_neg = dve_op(p, lambda e, o=sc2[0:1, 5:6], i=sc2[0:1, 4:5]:
                   e.tensor_scalar_mul(o, i, -1.0 / 32.0), [("act", c_sqr)])
    c_mmn = mm(p, G[3][:, 0:1], onr_sb[:, :], sc2[0:1, 5:6], True, True,
               [("dve", c_neg)], inc="pe")
    c_negb = act_evac(p, negm[:, :], G[3][:, 0:1], [("pe", c_mmn)])

    n_std = 0
    c_o_ev = None
    c_exp_prev = {}
    c_pv_par = {0: None, 1: None}
    for qc in range(2):
        c_exp = {}
        for kt in range(32):
            bank = G[kt % 2]
            w = [("ld", c_k), ("ld", c_q)]
            if (qc, kt) != (0, 0):
                prev = c_exp_prev.get(kt % 2)
                if prev:
                    w.append(("act", prev))
            if qc == 1:
                w.append(("pe", c_pv_par[0] or 0))
                w.append(("pe", c_pv_par[1] or 0))
            c_mm = None
            for dc in range(8):
                c_mm = mm(p, bank[:, :], K_sb[:, dc * 4096 + kt * 128:dc * 4096 + kt * 128 + 128],
                          q_sb[:, dc * 1024 + qc * 512:dc * 1024 + qc * 512 + 512],
                          dc == 0, dc == 7, w if dc == 0 else [],
                          inc="pe" if dc == 7 else None)
            c_exp[kt] = act_evac(p, PT[:, kt * 512:(kt + 1) * 512], bank[:, :],
                                 [("pe", c_mm), ("act", c_negb)],
                                 func=AF.Exp, bias=negm[:, :], scale=1.0 / 32.0)
            c_exp_prev[kt % 2] = c_exp[kt]
        c_mm = None
        for kt in range(32):
            w = [("act", c_exp[kt])]
            if kt == 0 and c_o_ev is not None:
                w.append(("dve", c_o_ev))
            c_mm = mm(p, psV[0:1, :], onb_sb[:, :], PT[:, kt * 512:(kt + 1) * 512],
                      kt == 0, kt == 31, w, inc="pe" if kt == 31 else None)
        c_rcp = dve_op(p, lambda e, o=rcp_row[0:1, :], i=psV[0:1, :]: e.reciprocal(o, i),
                       [("pe", c_mm)])
        c_mm = mm(p, G[3][:, :], onr_sb[:, :], rcp_row[0:1, :], True, True,
                  [("dve", c_rcp), ("act", c_negb)], inc="pe")
        w = [("pe", c_mm)]
        if c_o_ev is not None:
            w.append(("dve", c_o_ev))
        c_rb = act_evac(p, recipb[:, :], G[3][:, :], w)
        for vc in range(8):
            vb = Vb[vc % 2]
            c_v = None
            for kt in range(32):
                wv = []
                if kt == 0 and c_pv_par[vc % 2] is not None:
                    wv = [("pe", c_pv_par[vc % 2])]
                c_v = p.dma("sync", vb[:, kt * 128:(kt + 1) * 128],
                            V_h[kt * 128:(kt + 1) * 128, vc * 128:(vc + 1) * 128], wv)
            c_mm = None
            for kt in range(32):
                w = [("ld", c_v), ("act", c_exp[31])]
                if kt == 0 and c_o_ev is not None:
                    w.append(("dve", c_o_ev))
                c_mm = mm(p, G[2][:, :], vb[:, kt * 128:(kt + 1) * 128],
                          PT[:, kt * 512:(kt + 1) * 512], kt == 0, kt == 31,
                          w if kt == 0 else [], inc="pe" if kt == 31 else None)
            c_pv_par[vc % 2] = c_mm
            c_o_ev = dve_op(p, lambda e, o=O_sb[:, vc * 1024 + qc * 512:vc * 1024 + qc * 512 + 512],
                            a=G[2][:, :], b=recipb[:, :]: e.tensor_mul(o, a, b),
                            [("pe", c_mm), ("act", c_rb)])

    attn_done_pe = p.cnt["pe"]
    attn_done_dve = p.cnt["dve"]
    attn_st.close()

    wst = st.enter_context(nc.sbuf_tensor("wst", [128, 8 * 512], F32))
    wbf = st.enter_context(nc.sbuf_tensor("wbf", [128, 8 * 512], BF16))
    xcol = st.enter_context(nc.sbuf_tensor("xcol", [128, 8 * 512], F32))
    fcol = st.enter_context(nc.sbuf_tensor("fcol", [128, 8 * 512], F32))
    borow = st.enter_context(nc.sbuf_tensor("borow", [1, 512], F32))
    bob = st.enter_context(nc.sbuf_tensor("bob", [128, 512], F32))
    stgf = [st.enter_context(nc.sbuf_tensor(f"stgf{i}", [128, 512], F32)) for i in range(2)]

    p.wait_only("sync", [("dve", attn_done_dve)])
    c_res_prev = None
    c_std_prev = {}
    c_bob = None
    for oc in range(8):
        c_cast = wload(p, None, wo_h, 8, oc * 512, 512, wst, wbf,
                       attn_done_pe if oc == 0 else p.cnt["pe"])
        c_xc = fc = None
        for rs in range(8):
            w = [("dve", c_res_prev)] if (oc > 0 and rs == 0 and c_res_prev) else []
            c_xc = p.dma("sync", xcol[:, rs * 512:(rs + 1) * 512],
                         x_h[rs * 128:(rs + 1) * 128, oc * 512:(oc + 1) * 512], w)
        for rs in range(8):
            fc = p.dma("sync", fcol[:, rs * 512:(rs + 1) * 512],
                       ffn_h[rs * 128:(rs + 1) * 128, oc * 512:(oc + 1) * 512])
        c_bo = p.dma("sync", borow[0:1, :], bo_h[oc * 512:(oc + 1) * 512])
        c_bob = bcast_row(p, G[3], bob[:, :], borow[0:1, :], onr_sb[:, :], 512,
                          [("ld", c_bo), ("dve", p.cnt["dve"])] +
                          ([("act", c_bob)] if c_bob else []))
        for rs in range(8):
            qc = rs // 4
            bank = G[rs % 2]; bi = rs % 2
            c_mm = None
            for dc in range(8):
                w = []
                if dc == 0:
                    w = [("pool", c_cast), ("dve", attn_done_dve)]
                    if bi in c_std_prev:
                        w.append(("dve", c_std_prev[bi]))
                c_mm = mm(p, bank[:, :],
                          O_sb[:, dc * 1024 + qc * 512 + (rs % 4) * 128:
                               dc * 1024 + qc * 512 + (rs % 4) * 128 + 128],
                          wbf[:, dc * 512:(dc + 1) * 512], dc == 0, dc == 7,
                          w, inc="pe" if dc == 7 else None)
            sf = stgf[rs % 2]
            w = [("pe", c_mm), ("act", c_bob), ("ld", c_xc), ("ld", fc)]
            if n_std >= 2:
                w.append(("std", (n_std - 1) * 16))
            dve_op(p, lambda e, o=sf[:, :], a=bank[:, :], b=bob[:, :]: e.tensor_add(o, a, b), w)
            dve_op(p, lambda e, o=sf[:, :], a=sf[:, :], b=xcol[:, rs * 512:(rs + 1) * 512]:
                   e.tensor_add(o, a, b))
            c_ev = dve_op(p, lambda e, o=sf[:, :], a=sf[:, :], b=fcol[:, rs * 512:(rs + 1) * 512]:
                          e.tensor_add(o, a, b))
            c_std_prev[bi] = c_ev
            c_res_prev = c_ev
            p.dma("act", x1_h[rs * 128:(rs + 1) * 128, oc * 512:(oc + 1) * 512], sf[:, :],
                  [("dve", c_ev)], inc="std")
            n_std += 1

    p.wait_only("sync", [("std", p.cnt["std"])])
    p.finalize()
    st.close()
    return nc


def build_C():
    nc = bass.Bass("TRN2", target_bir_lowering=False)
    st = ExitStack()
    x_h = nc.dram_tensor("x", [R, D], F32, kind="ExternalInput").ap()
    wqkv_h = nc.dram_tensor("wqkv", [D, 6144], F32, kind="ExternalInput").ap()
    bqkv_h = nc.dram_tensor("bqkv", [6144], F32, kind="ExternalInput").ap()
    bwt_h = nc.dram_tensor("bwt", [D, D], F32, kind="ExternalInput").ap()
    bb_h = nc.dram_tensor("bb", [D], F32, kind="ExternalInput").ap()
    at_h = nc.dram_tensor("at", [D, 128], F32, kind="ExternalInput").ap()
    ropec_h = nc.dram_tensor("ropec", [128, 1024], F32, kind="ExternalInput").ap()
    ropes_h = nc.dram_tensor("ropes", [128, 1024], F32, kind="ExternalInput").ap()
    pswap_h = nc.dram_tensor("pswap", [128, 128], F32, kind="ExternalInput").ap()
    mcol_h = nc.dram_tensor("mcol", [R], BF16, kind="ExternalInput").ap()
    idb_h = nc.dram_tensor("idb", [128, 128], BF16, kind="ExternalInput").ap()
    onr_h = nc.dram_tensor("onr", [1, 128], F32, kind="ExternalInput").ap()
    qT_h = nc.dram_tensor("qT", [4096, R], BF16, kind="ExternalOutput").ap()
    kT_h = nc.dram_tensor("kT", [1024, R], BF16, kind="ExternalOutput").ap()
    v_h = nc.dram_tensor("v", [R, 1024], BF16, kind="ExternalOutput").ap()
    base_h = nc.dram_tensor("base", [R, D], F32, kind="ExternalOutput").ap()
    h1_h = nc.dram_tensor("h1", [128, R], BF16, kind="ExternalOutput").ap()
    qsum_h = nc.dram_tensor("qsum", [1, D], F32, kind="ExternalOutput").ap()

    xt = st.enter_context(nc.sbuf_tensor("xt", [128, D], F32))
    fbuf = st.enter_context(nc.sbuf_tensor("fbuf", [128, D], F32))
    xn_tm = st.enter_context(nc.sbuf_tensor("xn_tm", [128, D], BF16))
    xn = st.enter_context(nc.sbuf_tensor("xn", [128, NDC * R], BF16))
    wst = st.enter_context(nc.sbuf_tensor("wst", [128, 8 * 512], F32))
    wbf = st.enter_context(nc.sbuf_tensor("wbf", [128, 32 * 512], BF16))
    stt = st.enter_context(nc.sbuf_tensor("stt", [128, 8], F32))
    bq = st.enter_context(nc.sbuf_tensor("bq", [128, 40], F32))
    vbrow = st.enter_context(nc.sbuf_tensor("vbrow", [1, 1024], F32))
    vbb = st.enter_context(nc.sbuf_tensor("vbb", [128, 1024], F32))
    idb_sb = st.enter_context(nc.sbuf_tensor("idbs", [128, 128], BF16))
    onr_sb = st.enter_context(nc.sbuf_tensor("onrs", [1, 128], F32))
    pswap_sb = st.enter_context(nc.sbuf_tensor("pswaps", [128, 128], F32))
    ropec_sb = st.enter_context(nc.sbuf_tensor("ropecs", [128, 1024], F32))
    ropes_sb = st.enter_context(nc.sbuf_tensor("ropess", [128, 1024], F32))
    mcol_sb = st.enter_context(nc.sbuf_tensor("mcols", [128, 8], BF16))
    at_bf = st.enter_context(nc.sbuf_tensor("atbf", [128, D], BF16))
    xsum = st.enter_context(nc.sbuf_tensor("xsum", [1, D], F32))
    stg = [st.enter_context(nc.sbuf_tensor(f"stg{i}", [128, 1024], BF16)) for i in range(2)]
    stgf = [st.enter_context(nc.sbuf_tensor(f"stgf{i}", [128, 512], F32)) for i in range(2)]
    stgv = [st.enter_context(nc.sbuf_tensor(f"stgv{i}", [128, 512], BF16)) for i in range(2)]
    bbrow = st.enter_context(nc.sbuf_tensor("bbrow", [1, 512], F32))
    bcb = st.enter_context(nc.sbuf_tensor("bcb", [128, 512], F32))

    G = [st.enter_context(nc.psum_tensor(f"G{i}", [128, 512], F32)) for i in range(4)]
    psT = st.enter_context(nc.psum_tensor("psT", [128, 128], BF16))
    psT2 = st.enter_context(nc.psum_tensor("psT2", [128, 128], BF16))
    psV = st.enter_context(nc.psum_tensor("psV", [128, 512], F32))

    p = PB(nc, st)
    p.dma("sync", idb_sb[:, :], idb_h[:, :])
    p.dma("sync", onr_sb[:, :], onr_h[:, :])
    p.dma("sync", pswap_sb[:, :], pswap_h[:, :])
    p.dma("sync", ropec_sb[:, :], ropec_h[:, :])
    p.dma("sync", ropes_sb[:, :], ropes_h[:, :])
    p.dma("sync", vbrow[:, :], bqkv_h[5120:6144])
    c_c = None
    for rt in range(8):
        p.dma("sync", mcol_sb[:, rt:rt + 1], mcol_h[rt * 128:(rt + 1) * 128])
    for of in range(40):
        c_c = p.dma("sync", bq[:, of:of + 1], bqkv_h[of * 128:(of + 1) * 128])
    c_ropetab = c_c
    c_vbb0 = bcast_row(p, G[2], vbb[:, 0:512], vbrow[0:1, 0:512], onr_sb[:, :], 512,
                       [("ld", c_c)])
    c_vbb = bcast_row(p, G[2], vbb[:, 512:1024], vbrow[0:1, 512:1024], onr_sb[:, :], 512,
                      [("act", c_vbb0)])

    c_xn_done, _ = layernorm_phase(p, x_h, xt, wst, xn_tm, xn_fm=xn, stt=stt,
                                   psT=psT, psT2=psT2, idb_sb=idb_sb,
                                   qsum=(mcol_sb, xsum, psV))
    p.dma("act", qsum_h[0:1, :], xsum[0:1, :], [("dve", p.cnt["dve"])], inc="std")

    c_war_pe = None
    n_store = 0
    c_rope_prev = None
    c_t2 = None
    for og in range(10):
        c_cast = wload(p, None, wqkv_h, NDC, og * 512, 512, wst, wbf, c_war_pe)
        for of_l in range(4):
            of = og * 4 + of_l
            c_ev = []
            for rh in range(2):
                bank = G[rh]; bi = rh
                c_mm = None
                for dc in range(NDC):
                    w = []
                    if dc == 0:
                        w = [("pool", c_cast), ("act", c_xn_done)]
                        if c_rope_prev is not None:
                            w.append(("dve", c_rope_prev))
                        if c_t2 is not None:
                            w.append(("pool", c_t2))
                    c_mm = mm(p, bank[:, :],
                              wbf[:, dc * 512 + of_l * 128:dc * 512 + of_l * 128 + 128],
                              xn[:, dc * 1024 + rh * 512:dc * 1024 + rh * 512 + 512],
                              dc == 0, dc == NDC - 1, w, inc="pe" if dc == NDC - 1 else None)
                c_ev.append(act_evac(p, fbuf[:, rh * 512:(rh + 1) * 512], bank[:, :],
                                     [("pe", c_mm)], bias=bq[:, of:of + 1]))
            c_xs = []
            for rh in range(2):
                w = [("act", c_ev[rh])]
                if c_t2 is not None:
                    w.append(("pool", c_t2))
                c_mm = mm(p, G[2 + rh][:, :], pswap_sb[:, :], fbuf[:, rh * 512:(rh + 1) * 512],
                          True, True, w, inc="pe")
                c_xs.append(act_evac(p, fbuf[:, 1024 + rh * 512:1024 + (rh + 1) * 512],
                                     G[2 + rh][:, :], [("pe", c_mm)]))
            c_t1 = dve_op(p, lambda e, o=fbuf[:, 2048:3072], a=fbuf[:, 0:1024],
                          b=ropec_sb[:, :]: e.tensor_mul(o, a, b),
                          [("act", c_ev[1]), ("ld", c_ropetab)])
            c_t2 = dve_op(p, lambda e, o=fbuf[:, 3072:4096], a=fbuf[:, 1024:2048],
                          b=ropes_sb[:, :]: e.tensor_mul(o, a, b),
                          [("act", c_xs[1]), ("ld", c_ropetab)], eng="pool", inc="pool")
            s = stg[of % 2]
            w = [("dve", c_t1), ("pool", c_t2)]
            if n_store >= 2:
                w.append(("sta", (n_store - 1) * 16))
            c_rope_prev = dve_op(p, lambda e, o=s[:, :], a=fbuf[:, 2048:3072],
                                 b=fbuf[:, 3072:4096]: e.tensor_add(o, a, b), w)
            dst = qT_h if of < 32 else kT_h
            p.dma("act", dst[(of % 32 if of < 32 else of - 32) * 128:
                             (of % 32 if of < 32 else of - 32) * 128 + 128, :],
                  s[:, :], [("dve", c_rope_prev)], inc="sta")
            n_store += 1
        c_war_pe = p.cnt["pe"]

    n_std = 1
    c_std_prev = {}
    for ogv in range(2):
        c_cast = wload(p, None, wqkv_h, NDC, 5120 + ogv * 512, 512, wst, wbf, c_war_pe)
        for rs in range(8):
            bank = G[rs % 2]; bi = rs % 2
            c_mm = None
            for dc in range(NDC):
                w = []
                if dc == 0:
                    w = [("pool", c_cast), ("act", c_vbb)]
                    if bi in c_std_prev:
                        w.append(("dve", c_std_prev[bi]))
                c_mm = mm(p, bank[:, :], xn[:, dc * 1024 + rs * 128:dc * 1024 + rs * 128 + 128],
                          wbf[:, dc * 512:(dc + 1) * 512], dc == 0, dc == NDC - 1,
                          w, inc="pe" if dc == NDC - 1 else None)
            sv = stgv[rs % 2]
            w = [("pe", c_mm)]
            if n_std >= 3:
                w.append(("std", (n_std - 1) * 16))
            c_ev = dve_op(p, lambda e, o=sv[:, :], a=bank[:, :], b=vbb[:, ogv * 512:(ogv + 1) * 512]:
                          e.tensor_add(o, a, b), w)
            c_std_prev[bi] = c_ev
            p.dma("act", v_h[rs * 128:(rs + 1) * 128, ogv * 512:(ogv + 1) * 512], sv[:, :],
                  [("dve", c_ev)], inc="std")
            n_std += 1
        c_war_pe = p.cnt["pe"]

    c_cast = wload(p, None, at_h, NDC, 0, 128, wst, wbf, c_war_pe)
    c_at = dve_op(p, lambda e, o=at_bf[:, :], i=wbf[:, 0:4096]: e.tensor_copy(o, i),
                  [("pool", c_cast)], eng="pool", inc="pool")
    c_h = None
    for rh in range(2):
        bank = G[3]
        c_mm = None
        for dc in range(NDC):
            w = [("pool", c_at)] if dc == 0 else []
            if dc == 0:
                w.append(("act", c_h if c_h is not None else p.cnt["act"]))
            c_mm = mm(p, bank[:, :], at_bf[:, dc * 128:(dc + 1) * 128],
                      xn[:, dc * 1024 + rh * 512:dc * 1024 + rh * 512 + 512],
                      dc == 0, dc == NDC - 1, w, inc="pe" if dc == NDC - 1 else None)
        c_h = act_evac(p, stg[0][:, rh * 512:(rh + 1) * 512], bank[:, :],
                       [("pe", c_mm), ("sta", p.cnt["sta"])])
    p.dma("act", h1_h[:, :], stg[0][:, :], [], inc="sta")

    c_bcb = None
    for oc in range(8):
        c_cast = wload(p, None, bwt_h, NDC, oc * 512, 512, wst, wbf, p.cnt["pe"])
        c_bb = p.dma("sync", bbrow[0:1, :], bb_h[oc * 512:(oc + 1) * 512],
                     [("act", c_bcb)] if c_bcb else [])
        c_bcb = bcast_row(p, G[2], bcb[:, :], bbrow[0:1, :],
                          onr_sb[:, :], 512,
                          [("ld", c_bb), ("dve", p.cnt["dve"]), ("act", p.cnt["act"])])
        for rs in range(8):
            bank = G[rs % 2]; bi = rs % 2
            c_mm = None
            for dc in range(NDC):
                w = []
                if dc == 0:
                    w = [("pool", c_cast)]
                    if bi in c_std_prev:
                        w.append(("dve", c_std_prev[bi]))
                c_mm = mm(p, bank[:, :], xn[:, dc * 1024 + rs * 128:dc * 1024 + rs * 128 + 128],
                          wbf[:, dc * 512:(dc + 1) * 512], dc == 0, dc == NDC - 1,
                          w, inc="pe" if dc == NDC - 1 else None)
            sf = stgf[rs % 2]
            w = [("pe", c_mm), ("act", c_bcb)]
            if n_std >= 3:
                w.append(("std", (n_std - 1) * 16))
            c_ev = dve_op(p, lambda e, o=sf[:, :], a=bank[:, :], b=bcb[:, :]:
                          e.tensor_add(o, a, b), w)
            c_std_prev[bi] = c_ev
            p.dma("act", base_h[rs * 128:(rs + 1) * 128, oc * 512:(oc + 1) * 512], sf[:, :],
                  [("dve", c_ev)], inc="std")
            n_std += 1

    p.wait_only("sync", [("sta", p.cnt["sta"]), ("std", p.cnt["std"])])
    p.finalize()
    st.close()
    return nc


def build_D():
    nc = bass.Bass("TRN2", target_bir_lowering=False)
    st = ExitStack()
    x1_h = nc.dram_tensor("x1", [R, D], F32, kind="ExternalInput").ap()
    qT_h = nc.dram_tensor("qT", [4096, R], BF16, kind="ExternalInput").ap()
    KT_h = nc.dram_tensor("KT", [1024, 4096], BF16, kind="ExternalInput").ap()
    V_h = nc.dram_tensor("V", [4096, 1024], BF16, kind="ExternalInput").ap()
    wo_h = nc.dram_tensor("wo", [D, D], F32, kind="ExternalInput").ap()
    bo_h = nc.dram_tensor("bo", [D], F32, kind="ExternalInput").ap()
    base_h = nc.dram_tensor("base", [R, D], F32, kind="ExternalInput").ap()
    h1_h = nc.dram_tensor("h1", [128, R], BF16, kind="ExternalInput").ap()
    qsums_h = nc.dram_tensor("qsums", [4, D], F32, kind="ExternalInput").ap()
    bfl_h = nc.dram_tensor("bfl", [128, D], F32, kind="ExternalInput").ap()
    rwt_h = nc.dram_tensor("rwt", [D, 8], F32, kind="ExternalInput").ap()
    rb_h = nc.dram_tensor("rb", [8], F32, kind="ExternalInput").ap()
    mask_h = nc.dram_tensor("mask", [4096, R], BF16, kind="ExternalInput").ap()
    idf_h = nc.dram_tensor("idf", [128, 128], F32, kind="ExternalInput").ap()
    onr_h = nc.dram_tensor("onr", [1, 128], F32, kind="ExternalInput").ap()
    onb_h = nc.dram_tensor("onb", [128, 1], BF16, kind="ExternalInput").ap()
    onc_h = nc.dram_tensor("onc", [128, 1], F32, kind="ExternalInput").ap()
    esel_h = nc.dram_tensor("esel", [8, 128], F32, kind="ExternalInput").ap()
    out_h = nc.dram_tensor("out", [R, D], F32, kind="ExternalOutput").ap()

    onc_sb = st.enter_context(nc.sbuf_tensor("oncs", [128, 1], F32))
    onr_sb = st.enter_context(nc.sbuf_tensor("onrs", [1, 128], F32))
    onb_sb = st.enter_context(nc.sbuf_tensor("onbs", [128, 1], BF16))
    idf_sb = st.enter_context(nc.sbuf_tensor("idfs", [128, 128], F32))
    esel_sb = st.enter_context(nc.sbuf_tensor("esels", [8, 128], F32))
    O_sb = st.enter_context(nc.sbuf_tensor("Osb", [128, 32 * 1024], BF16))
    hw1 = st.enter_context(nc.sbuf_tensor("hw1", [128, R], BF16))
    sc2 = st.enter_context(nc.sbuf_tensor("sc2", [1, 8], F32))
    negm = st.enter_context(nc.sbuf_tensor("negm", [128, 1], F32))
    recipb = st.enter_context(nc.sbuf_tensor("recipb", [128, 512], F32))
    rcp_row = st.enter_context(nc.sbuf_tensor("rcprow", [1, 512], F32))
    smal = st.enter_context(nc.sbuf_tensor("smal", [1, 1024], F32))
    rt_st = ExitStack()
    h1_sb = rt_st.enter_context(nc.sbuf_tensor("h1s", [128, R], BF16))
    qrow = rt_st.enter_context(nc.sbuf_tensor("qrow", [1, D], F32))
    qfm = rt_st.enter_context(nc.sbuf_tensor("qfm", [128, 32], F32))
    rwt_sb = rt_st.enter_context(nc.sbuf_tensor("rwts", [128, 8 * 32], F32))
    rb_sb = rt_st.enter_context(nc.sbuf_tensor("rbs", [8, 1], F32))
    lgT = rt_st.enter_context(nc.sbuf_tensor("lgT", [8, 1], F32))
    lg8 = rt_st.enter_context(nc.sbuf_tensor("lg8", [1, 8], F32))
    pr8 = rt_st.enter_context(nc.sbuf_tensor("pr8", [1, 8], F32))
    r8 = rt_st.enter_context(nc.sbuf_tensor("r8", [8, 1], F32))
    wex1 = rt_st.enter_context(nc.sbuf_tensor("wex1", [128, 1], F32))
    qs_sb = rt_st.enter_context(nc.sbuf_tensor("qss", [4, D], F32))

    G = [st.enter_context(nc.psum_tensor(f"G{i}", [128, 512], F32)) for i in range(4)]
    psV = st.enter_context(nc.psum_tensor("psV", [128, 512], F32))
    psTf = st.enter_context(nc.psum_tensor("psTf", [128, 128], F32))

    p = PB(nc, st)
    p.dma("sync", onr_sb[:, :], onr_h[:, :])
    p.dma("sync", onb_sb[:, :], onb_h[:, :])
    p.dma("sync", onc_sb[:, :], onc_h[:, :])
    p.dma("sync", idf_sb[:, :], idf_h[:, :])
    p.dma("sync", esel_sb[:, :], esel_h[:, :])
    p.dma("sync", rb_sb[:, 0:1], rb_h[:])
    p.dma("sync", h1_sb[:, :], h1_h[:, :])
    c_qs = p.dma("sync", qs_sb[:, :], qsums_h[:, :])
    c_rw = None
    for dc in range(NDC):
        c_rw = p.dma("sync", rwt_sb[:, dc * 8:(dc + 1) * 8], rwt_h[dc * 128:(dc + 1) * 128, :])

    c_mm = None
    c_ev = None
    for ch in range(8):
        w = [("ld", c_qs)] if ch == 0 else [("act", c_ev)]
        c_mm = mm(p, psV[0:1, :], onc_sb[0:4, 0:1],
                  qs_sb[:, ch * 512:(ch + 1) * 512], True, True, w,
                  inc="pe")
        c_ev = act_evac(p, qrow[0:1, ch * 512:(ch + 1) * 512], psV[0:1, :], [("pe", c_mm)])
    c_qrow = c_ev
    c_qf = None
    for c in range(32):
        c_qf = p.dma("sync", qfm[:, c:c + 1], qrow[0:1, c * 128:(c + 1) * 128],
                     [("act", c_qrow)])
    c_mm = None
    for c in range(32):
        c_mm = mm(p, G[0][0:8, 0:1], rwt_sb[:, c * 8:(c + 1) * 8], qfm[:, c:c + 1],
                  c == 0, c == 31,
                  [("ld", c_qf), ("ld", c_rw)] if c == 0 else [],
                  inc="pe" if c == 31 else None)
    c_lgT = act_evac(p, lgT[:, :], G[0][0:8, 0:1], [("pe", c_mm)],
                     bias=rb_sb[:, 0:1], scale=1.0 / 3484.0)
    c_tr = p.emit("pe", lambda e, o=psTf[0:1, 0:8], i=lgT[:, :], idn=idf_sb[0:8, 0:8]:
                  e.transpose(o, i, idn), [("act", c_lgT)], "pe")
    c_lg8 = act_evac(p, lg8[:, :], psTf[0:1, 0:8], [("pe", c_tr)])
    dve_op(p, lambda e, o=sc2[0:1, 0:1], i=lg8[0:1, :]: e.reduce_max(o, i, axis=AX),
           [("act", c_lg8)])
    c_nm = dve_op(p, lambda e, o=sc2[0:1, 1:2], i=sc2[0:1, 0:1]: e.tensor_scalar_mul(o, i, -1.0))
    c_eu = act_evac(p, pr8[:, :], lg8[:, :], [("dve", c_nm)], func=AF.Exp, bias=sc2[0:1, 1:2])
    dve_op(p, lambda e, o=sc2[0:1, 2:3], i=pr8[0:1, :]: e.reduce_sum(o, i, axis=AX),
           [("act", c_eu)])
    dve_op(p, lambda e, o=sc2[0:1, 3:4], i=sc2[0:1, 2:3]: e.reciprocal(o, i))
    c_pr = dve_op(p, lambda e, o=pr8[:, :], i=pr8[:, :], s=sc2[0:1, 3:4]:
                  e.tensor_scalar_mul(o, i, s))
    c_tr = p.emit("pe", lambda e, o=psTf[0:8, 0:1], i=pr8[0:1, :], idn=idf_sb[0:1, 0:1]:
                  e.transpose(o, i, idn), [("dve", c_pr)], "pe")
    c_r8 = act_evac(p, r8[:, :], psTf[0:8, 0:1], [("pe", c_tr)])
    c_mm = mm(p, G[0][:, 0:1], esel_sb[:, :], r8[:, 0:1], True, True, [("act", c_r8)], inc="pe")
    c_wex = act_evac(p, wex1[:, :], G[0][:, 0:1], [("pe", c_mm)])
    c_hw1 = dve_op(p, lambda e, o=hw1[:, :], i=h1_sb[:, :], s=wex1[:, :]:
                   e.tensor_scalar_mul(o, i, s), [("act", c_wex), ("ld", c_qs)])

    rt_st.close()
    attn_st = ExitStack()
    PT = attn_st.enter_context(nc.sbuf_tensor("PT", [128, 32 * 512], BF16))
    msk = attn_st.enter_context(nc.sbuf_tensor("msk", [128, 32 * 512], BF16))
    kh = [attn_st.enter_context(nc.sbuf_tensor(f"kh{i}", [128, 4096], BF16)) for i in range(2)]
    vh = [attn_st.enter_context(nc.sbuf_tensor(f"vh{i}", [128, 32 * 128], BF16)) for i in range(2)]
    qh = [attn_st.enter_context(nc.sbuf_tensor(f"qh{i}", [128, 512], BF16)) for i in range(2)]
    sqb = attn_st.enter_context(nc.sbuf_tensor("sqb", [128, 1024], BF16))
    qn = attn_st.enter_context(nc.sbuf_tensor("qn", [128, 1024], BF16))

    c_prev = None
    c_mq = c_mk = None
    for t in range(40):
        nch = 2 if t < 32 else 8
        if t < 32:
            buf = qn
            c_l = p.dma("sync", buf[:, :], qT_h[t * 128:(t + 1) * 128, :],
                        ([("act", c_prev)] if c_prev else []) +
                        ([("dve", c_hw1)] if t == 0 else []))
            c_prev = act_evac(p, sqb[:, :], buf[:, :],
                              [("ld", c_l), ("pe", p.cnt["pe"])], func=AF.Square)
            sqv = sqb
        else:
            buf = kh[0]
            c_l = p.dma("sync", buf[:, :], KT_h[(t - 32) * 128:(t - 32) * 128 + 128, :],
                        [("act", c_prev)] if c_prev else [])
            c_prev = act_evac(p, kh[1][:, :], buf[:, :],
                              [("ld", c_l), ("pe", p.cnt["pe"])], func=AF.Square)
            sqv = kh[1]
        for half in range(nch):
            w = [("act", c_prev)]
            if c_mq is not None:
                w.append(("dve", c_mq if t < 32 else (c_mk or c_mq)))
            c_mmn = mm(p, psV[0:1, :], onb_sb[:, :], sqv[:, half * 512:(half + 1) * 512],
                       True, True, w, inc="pe")
            c_red = dve_op(p, lambda e, o=sc2[0:1, 4:5], i=psV[0:1, :]:
                           e.reduce_max(o, i, axis=AX), [("pe", c_mmn)])
            idx = 5 if t < 32 else 6
            if (t == 0 and half == 0) or (t == 32 and half == 0):
                cc = dve_op(p, lambda e, o=sc2[0:1, idx:idx + 1], i=sc2[0:1, 4:5]:
                            e.tensor_copy(o, i))
            else:
                cc = dve_op(p, lambda e, o=sc2[0:1, idx:idx + 1], a=sc2[0:1, idx:idx + 1],
                            b=sc2[0:1, 4:5]: e.tensor_max(o, a, b))
            if t < 32:
                c_mq = cc
            else:
                c_mk = cc
    SCALE1 = 1.0 / float(np.sqrt(np.float64(128.0)))
    c_m2 = dve_op(p, lambda e, o=sc2[0:1, 0:1], a=sc2[0:1, 5:6], b=sc2[0:1, 6:7]:
                  e.tensor_mul(o, a, b), [("dve", c_mk)])
    c_sqr = act_evac(p, sc2[0:1, 1:2], sc2[0:1, 0:1], [("dve", c_m2)], func=AF.Sqrt)
    c_neg = dve_op(p, lambda e, o=sc2[0:1, 2:3], i=sc2[0:1, 1:2]:
                   e.tensor_scalar_mul(o, i, -SCALE1), [("act", c_sqr)])
    c_mmn = mm(p, G[3][:, 0:1], onr_sb[:, :], sc2[0:1, 2:3], True, True,
               [("dve", c_neg)], inc="pe")
    c_negb = act_evac(p, negm[:, :], G[3][:, 0:1], [("pe", c_mmn)])
    c_o_ev = None
    c_exp_prev = {}
    c_pv_done = None
    for qc in range(2):
        c_msk = None
        for kt in range(32):
            c_msk = p.dma("sync", msk[:, kt * 512:(kt + 1) * 512],
                          mask_h[kt * 128:(kt + 1) * 128, qc * 512:(qc + 1) * 512],
                          [("pool", p.cnt["pool"]), ("dve", p.cnt["dve"])] if kt == 0 and qc == 1 else [])
        for kvh in range(8):
            kb = kh[kvh % 2]
            vb = vh[kvh % 2]
            w_k = [("pe", c_pv_done)] if c_pv_done else []
            if qc == 0 and kvh == 0:
                w_k = [("pe", p.cnt["pe"]), ("act", c_negb)]
            c_kl = p.dma("sync", kb[:, :], KT_h[kvh * 128:(kvh + 1) * 128, :], w_k)
            c_vl = None
            for kt in range(32):
                c_vl = p.dma("sync", vb[:, kt * 128:(kt + 1) * 128],
                             V_h[kt * 128:(kt + 1) * 128, kvh * 128:(kvh + 1) * 128])
            for qx in range(4):
                hq = kvh * 4 + qx
                qb = qh[hq % 2]
                c_ql = p.dma("sync", qb[:, :], qT_h[hq * 128:(hq + 1) * 128, qc * 512:(qc + 1) * 512],
                             [("pe", c_pv_done)] if c_pv_done else [])
                c_exp = {}
                c_mask = {}
                for kt in range(32):
                    bank = G[kt % 2]
                    w = [("ld", c_ql), ("ld", c_kl)]
                    prev = c_exp_prev.get(kt % 2)
                    if prev:
                        w.append(("act", prev))
                    c_mm = mm(p, bank[:, :], kb[:, kt * 128:(kt + 1) * 128], qb[:, :],
                              True, True, w, inc="pe")
                    c_exp[kt] = act_evac(p, PT[:, kt * 512:(kt + 1) * 512], bank[:, :],
                                         [("pe", c_mm), ("act", c_negb), ("pe", c_pv_done)],
                                         func=AF.Exp, bias=negm[:, :], scale=SCALE1)
                    c_exp_prev[kt % 2] = c_exp[kt]
                    eng = "dve" if kt % 2 == 0 else "pool"
                    c_mask[kt] = dve_op(p, lambda e, o=PT[:, kt * 512:(kt + 1) * 512],
                                        a=PT[:, kt * 512:(kt + 1) * 512],
                                        b=msk[:, kt * 512:(kt + 1) * 512]:
                                        e.tensor_mul(o, a, b),
                                        [("act", c_exp[kt]), ("ld", c_msk)], eng=eng, inc=eng)
                c_mm = None
                for kt in range(32):
                    w = [("dve", c_mask[kt])] if kt % 2 == 0 else [("pool", c_mask[kt])]
                    if kt == 0 and c_o_ev is not None:
                        w.append(("dve", c_o_ev))
                    c_mm = mm(p, psV[0:1, :], onb_sb[:, :], PT[:, kt * 512:(kt + 1) * 512],
                              kt == 0, kt == 31, w, inc="pe" if kt == 31 else None)
                c_rcp = dve_op(p, lambda e, o=rcp_row[0:1, :], i=psV[0:1, :]:
                               e.reciprocal(o, i), [("pe", c_mm)])
                c_mm = mm(p, G[3][:, :], onr_sb[:, :], rcp_row[0:1, :], True, True,
                          [("dve", c_rcp), ("act", c_negb)], inc="pe")
                w = [("pe", c_mm)]
                if c_o_ev is not None:
                    w.append(("dve", c_o_ev))
                c_rb2 = act_evac(p, recipb[:, :], G[3][:, :], w)
                c_mm = None
                for kt in range(32):
                    w = [("ld", c_vl)]
                    if kt == 0 and c_o_ev is not None:
                        w.append(("dve", c_o_ev))
                    c_mm = mm(p, G[2][:, :], vb[:, kt * 128:(kt + 1) * 128],
                              PT[:, kt * 512:(kt + 1) * 512], kt == 0, kt == 31,
                              w if kt == 0 else [], inc="pe" if kt == 31 else None)
                c_pv_done = c_mm
                c_o_ev = dve_op(p, lambda e,
                                o=O_sb[:, hq * 1024 + qc * 512:hq * 1024 + qc * 512 + 512],
                                a=G[2][:, :], b=recipb[:, :]: e.tensor_mul(o, a, b),
                                [("pe", c_mm), ("act", c_rb2)])

    attn_pe = p.cnt["pe"]
    attn_dve = p.cnt["dve"]
    attn_st.close()

    wst = st.enter_context(nc.sbuf_tensor("wst", [128, 8 * 512], F32))
    wbf = st.enter_context(nc.sbuf_tensor("wbf", [128, 32 * 512], BF16))
    bfst = st.enter_context(nc.sbuf_tensor("bfst", [128, D], F32))
    bfl_bf = st.enter_context(nc.sbuf_tensor("bflbf", [128, D], BF16))
    xcol = st.enter_context(nc.sbuf_tensor("xcol", [128, 8 * 512], F32))
    bcol = st.enter_context(nc.sbuf_tensor("bcol", [128, 8 * 512], F32))
    borow = st.enter_context(nc.sbuf_tensor("borow", [1, 512], F32))
    bob = st.enter_context(nc.sbuf_tensor("bob", [128, 512], F32))
    stgf = [st.enter_context(nc.sbuf_tensor(f"stgf{i}", [128, 512], F32)) for i in range(2)]

    p.wait_only("sync", [("dve", attn_dve)])
    c_bf = None
    for i in range(4):
        c_bf = p.dma("sync", bfst[:, i * 1024:(i + 1) * 1024], bfl_h[:, i * 1024:(i + 1) * 1024])
    c_bfl = dve_op(p, lambda e, o=bfl_bf[:, :], i=bfst[:, :]: e.tensor_copy(o, i),
                   [("ld", c_bf)], eng="pool", inc="pool")

    n_std = 0
    c_res_prev = None
    c_std_prev = {}
    c_bob = None
    for oc in range(8):
        c_cast = wload(p, None, wo_h, NDC, oc * 512, 512, wst, wbf,
                       attn_pe if oc == 0 else p.cnt["pe"])
        c_xc = bc = None
        for rs in range(8):
            w = [("dve", c_res_prev)] if (oc > 0 and rs == 0 and c_res_prev) else []
            c_xc = p.dma("sync", xcol[:, rs * 512:(rs + 1) * 512],
                         x1_h[rs * 128:(rs + 1) * 128, oc * 512:(oc + 1) * 512], w)
        for rs in range(8):
            bc = p.dma("sync", bcol[:, rs * 512:(rs + 1) * 512],
                       base_h[rs * 128:(rs + 1) * 128, oc * 512:(oc + 1) * 512])
        c_bo = p.dma("sync", borow[0:1, :], bo_h[oc * 512:(oc + 1) * 512])
        c_bob = bcast_row(p, G[3], bob[:, :], borow[0:1, :], onr_sb[:, :], 512,
                          [("ld", c_bo), ("dve", p.cnt["dve"])] +
                          ([("act", c_bob)] if c_bob else []))
        for rs in range(8):
            qc = rs // 4
            bank = G[rs % 2]; bi = rs % 2
            c_mm = None
            for dc in range(NDC):
                w = []
                if dc == 0:
                    w = [("pool", c_cast), ("dve", c_hw1), ("dve", attn_dve)]
                    if bi in c_std_prev:
                        w.append(("dve", c_std_prev[bi]))
                c_mm = mm(p, bank[:, :],
                          O_sb[:, dc * 1024 + qc * 512 + (rs % 4) * 128:
                               dc * 1024 + qc * 512 + (rs % 4) * 128 + 128],
                          wbf[:, dc * 512:(dc + 1) * 512], dc == 0, False, w)
            c_mm = mm(p, bank[:, :], hw1[:, rs * 128:(rs + 1) * 128],
                      bfl_bf[:, oc * 512:(oc + 1) * 512], False, True,
                      [("pool", c_bfl)], inc="pe")
            sf = stgf[rs % 2]
            w = [("pe", c_mm), ("act", c_bob), ("ld", c_xc), ("ld", bc)]
            if n_std >= 2:
                w.append(("std", (n_std - 1) * 16))
            dve_op(p, lambda e, o=sf[:, :], a=bank[:, :], b=bob[:, :]: e.tensor_add(o, a, b), w)
            dve_op(p, lambda e, o=sf[:, :], a=sf[:, :], b=xcol[:, rs * 512:(rs + 1) * 512]:
                   e.tensor_add(o, a, b))
            c_ev = dve_op(p, lambda e, o=sf[:, :], a=sf[:, :], b=bcol[:, rs * 512:(rs + 1) * 512]:
                          e.tensor_add(o, a, b))
            c_std_prev[bi] = c_ev
            c_res_prev = c_ev
            p.dma("act", out_h[rs * 128:(rs + 1) * 128, oc * 512:(oc + 1) * 512], sf[:, :],
                  [("dve", c_ev)], inc="std")
            n_std += 1

    p.wait_only("sync", [("std", p.cnt["std"])])
    p.finalize()
    st.close()
    return nc


def _consts():
    idb = np.eye(128, dtype=BF)
    idf = np.eye(128, dtype=np.float32)
    onr = np.ones((1, 128), dtype=np.float32)
    onb = np.ones((128, 1), dtype=BF)
    psw = np.zeros((128, 128), dtype=np.float32)
    for i in range(128):
        psw[i, i ^ 1] = 1.0
    esel = np.zeros((8, 128), dtype=np.float32)
    for m in range(128):
        esel[m // 16, m] = 2.0
    inv = (1.0 / (np.float32(10000.0) ** (np.arange(0, 128, 2, dtype=np.float32)
                                          / np.float32(128.0)))).astype(np.float32)
    pos = np.arange(4096, dtype=np.float32)
    ang = inv[:, None] * pos[None, :]
    cosv = np.cos(ang).astype(np.float32)
    sinv = np.sin(ang).astype(np.float32)
    ropec = np.empty((128, 4096), dtype=np.float32)
    ropes = np.empty((128, 4096), dtype=np.float32)
    ropec[0::2, :] = cosv
    ropec[1::2, :] = cosv
    ropes[0::2, :] = -sinv
    ropes[1::2, :] = sinv
    return idb, idf, onr, onb, psw, esel, ropec, ropes


def kernel(**inp):
    f32 = np.float32
    idb, idf, onr, onb, psw, esel, ropec, ropes = _consts()

    x = np.ascontiguousarray(inp["L0_attn_x"], dtype=f32)
    wqkv0 = np.ascontiguousarray(inp["L0_attn_Wqkv"], dtype=f32)
    bqkv0 = np.ascontiguousarray(inp["L0_attn_bqkv"], dtype=f32)
    wo0 = np.ascontiguousarray(inp["L0_attn_Wo"], dtype=f32)
    bo0 = np.ascontiguousarray(inp["L0_attn_bo"], dtype=f32)
    bwt0 = np.ascontiguousarray(inp["L0_ffn_base_W"].T, dtype=f32)
    bb0 = np.ascontiguousarray(inp["L0_ffn_base_b"], dtype=f32)
    rwt0 = np.ascontiguousarray(inp["L0_ffn_router_W"].T, dtype=f32)
    rb0 = np.ascontiguousarray(inp["L0_ffn_router_b"], dtype=f32)
    at0 = np.ascontiguousarray(inp["L0_ffn_A"].reshape(128, 4096).T, dtype=f32)
    bfl0 = np.ascontiguousarray(inp["L0_ffn_B"].transpose(0, 2, 1).reshape(128, 4096), dtype=f32)
    em0 = np.ascontiguousarray(inp["L0_ffn_expert_map"], dtype=np.int32)
    w1 = np.ascontiguousarray(inp["L1_attn_w_qkv"].T, dtype=f32)
    b1 = np.ascontiguousarray(inp["L1_attn_b_qkv"], dtype=f32)
    wo1 = np.ascontiguousarray(inp["L1_attn_w_o"].T, dtype=f32)
    bo1 = np.ascontiguousarray(inp["L1_attn_b_o"], dtype=f32)
    bwt1 = np.ascontiguousarray(inp["L1_ffn_W"].T, dtype=f32)
    bb1 = np.ascontiguousarray(inp["L1_ffn_b"], dtype=f32)
    at1 = np.ascontiguousarray(inp["L1_ffn_A"].reshape(128, 4096).T, dtype=f32)
    bfl1 = np.ascontiguousarray(inp["L1_ffn_B"].transpose(0, 2, 1).reshape(128, 4096), dtype=f32)
    rwt1 = np.ascontiguousarray(inp["L1_ffn_router_W"].T, dtype=f32)
    rb1 = np.ascontiguousarray(inp["L1_ffn_router_b"], dtype=f32)

    cores = list(range(NCORE))
    rows = [(c // 4, c % 4) for c in cores]

    ncA = build_A()
    in_maps = []
    for (b, j) in rows:
        in_maps.append(dict(
            x=np.ascontiguousarray(x[b, j * R:(j + 1) * R, :]),
            wqkv=wqkv0, bqkv=bqkv0, bwt=bwt0, bb=bb0, rwt=rwt0, rb=rb0,
            at=at0, bfl=bfl0, em=em0, idb=idb, idf=idf, onr=onr, esel=esel))
    resA = run_bass_kernel_spmd(ncA, in_maps, core_ids=cores)
    qT0 = [resA.results[c]["qT"] for c in cores]
    kT0 = [resA.results[c]["kT"] for c in cores]
    v0 = [resA.results[c]["v"] for c in cores]
    ffn0 = [resA.results[c]["ffn"] for c in cores]

    KT0 = []
    V0 = []
    for b in range(2):
        K = np.concatenate([kT0[4 * b + j] for j in range(4)], axis=1)
        V = np.concatenate([v0[4 * b + j] for j in range(4)], axis=0)
        KT0.append(np.ascontiguousarray(K))
        V0.append(np.ascontiguousarray(V))

    ncB = build_B()
    in_maps = []
    for c, (b, j) in enumerate(rows):
        in_maps.append(dict(
            x=np.ascontiguousarray(x[b, j * R:(j + 1) * R, :]),
            qT=qT0[c], KT=KT0[b], V=V0[b], wo=wo0, bo=bo0, ffn=ffn0[c],
            onr=onr, onb=onb))
    resB = run_bass_kernel_spmd(ncB, in_maps, core_ids=cores)
    x1 = [resB.results[c]["x1"] for c in cores]

    ncC = build_C()
    in_maps = []
    for c, (b, j) in enumerate(rows):
        mcol = np.zeros((R,), dtype=BF)
        for r_ in range(R):
            g = j * R + r_
            if 611 <= g <= 4094:
                mcol[r_] = 1.0
        in_maps.append(dict(
            x=x1[c], wqkv=w1, bqkv=b1, bwt=bwt1, bb=bb1, at=at1,
            ropec=np.ascontiguousarray(ropec[:, j * R:(j + 1) * R]),
            ropes=np.ascontiguousarray(ropes[:, j * R:(j + 1) * R]),
            pswap=psw, mcol=mcol, idb=idb, onr=onr))
    resC = run_bass_kernel_spmd(ncC, in_maps, core_ids=cores)
    qT1 = [resC.results[c]["qT"] for c in cores]
    kT1 = [resC.results[c]["kT"] for c in cores]
    v1 = [resC.results[c]["v"] for c in cores]
    base1 = [resC.results[c]["base"] for c in cores]
    h1 = [resC.results[c]["h1"] for c in cores]
    qsum1 = [resC.results[c]["qsum"] for c in cores]

    KT1 = []
    V1 = []
    QS = []
    for b in range(2):
        KT1.append(np.ascontiguousarray(np.concatenate([kT1[4 * b + j] for j in range(4)], axis=1)))
        V1.append(np.ascontiguousarray(np.concatenate([v1[4 * b + j] for j in range(4)], axis=0)))
        QS.append(np.ascontiguousarray(np.concatenate([qsum1[4 * b + j] for j in range(4)], axis=0)))

    ncD = build_D()
    in_maps = []
    for c, (b, j) in enumerate(rows):
        maskc = np.zeros((4096, R), dtype=BF)
        kk = np.arange(4096)[:, None]
        qq = j * R + np.arange(R)[None, :]
        maskc[:] = (kk <= qq).astype(BF)
        in_maps.append(dict(
            x1=x1[c], qT=qT1[c], KT=KT1[b], V=V1[b], wo=wo1, bo=bo1,
            base=base1[c], h1=h1[c], qsums=QS[b], bfl=bfl1, rwt=rwt1, rb=rb1,
            mask=maskc, idf=idf, onr=onr, onb=onb,
            onc=np.ones((128, 1), dtype=np.float32), esel=esel))
    resD = run_bass_kernel_spmd(ncD, in_maps, core_ids=cores)
    out = np.empty((2, 4096, 4096), dtype=f32)
    for c, (b, j) in enumerate(rows):
        out[b, j * R:(j + 1) * R, :] = resD.results[c]["out"]
    return out


# version 04 (final)
# speedup vs baseline: 1.0275x; 1.0275x over previous
import numpy as np
import ml_dtypes
from contextlib import ExitStack

import concourse.bass as bass
import concourse.mybir as mybir
from concourse.bass_utils import run_bass_kernel_spmd

BF = ml_dtypes.bfloat16
F32 = mybir.dt.float32
BF16 = mybir.dt.bfloat16
I32 = mybir.dt.int32
AF = mybir.ActivationFunctionType
OP = mybir.AluOpType
AX = mybir.AxisListType.X

NCORE = 8
R = 1024
D = 4096
NDC = 32


class PB:
    ENGS = ("sync", "pe", "act", "dve", "pool")
    SEMS = ("ld", "ld2", "sta", "std", "pe", "act", "dve", "pool")

    def __init__(self, nc, stack):
        self.nc = nc
        self.steps = {e: [] for e in self.ENGS}
        self.cnt = {s: 0 for s in self.SEMS}
        self.hw = {e: {} for e in self.ENGS}
        self.sems = {s: stack.enter_context(nc.semaphore(s)) for s in self.SEMS}

    SERIAL = {"act": "act", "dve": "dve", "pool": "pool"}

    def emit(self, eng, fn, waits=(), inc=None, amt=1):
        waits = list(waits)
        own = self.SERIAL.get(eng)
        if own is not None and fn is not None:
            waits.append((own, self.cnt[own]))
        w = []
        for (s, v) in waits:
            if v is None or v <= 0:
                continue
            if v > self.hw[eng].get(s, 0):
                self.hw[eng][s] = v
                w.append((s, v))
        self.steps[eng].append((w, fn, inc, amt))
        if inc is not None:
            self.cnt[inc] += amt
            return self.cnt[inc]
        return None

    def wait_only(self, eng, waits):
        self.emit(eng, None, waits)

    def dma(self, eng, out_ap, in_ap, waits=(), inc="ld"):
        def f(e, o=out_ap, i=in_ap):
            return e.dma_start(out=o, in_=i)
        return self.emit(eng, f, waits, inc, 16)

    def finalize(self):
        with self.nc.Block() as blk:
            meths = {"sync": blk.sync, "pe": blk.tensor, "act": blk.scalar,
                     "dve": blk.vector, "pool": blk.gpsimd}
            for ename in self.ENGS:
                def runner(eng, _steps=self.steps[ename]):
                    for w, fn, inc, amt in _steps:
                        for (s, v) in w:
                            eng.wait_ge(self.sems[s], v)
                        if fn is None:
                            continue
                        ins = fn(eng)
                        if inc is not None:
                            ins.then_inc(self.sems[inc], amt)
                meths[ename](runner)


def mm(p, out_ap, lhsT_ap, rhs_ap, start, stop, waits=(), inc=None):
    def f(e, o=out_ap, l=lhsT_ap, r=rhs_ap, s=start, t=stop):
        return e.matmul(o, l, r, start=s, stop=t)
    return p.emit("pe", f, waits, inc)


def wload(p, w_ap, nrows_dc, col0, cols, wst, wst2, wbf, war_pe, ret_all=False):
    rounds = (nrows_dc + 7) // 8
    casts = []
    c_pool0 = p.cnt["pool"]
    c_dve0 = p.cnt["dve"]
    for r_ in range(rounds):
        stage = wst if r_ % 2 == 0 else wst2
        ndc = min(8, nrows_dc - r_ * 8)
        c_ld = c_ld2 = None
        for i in range(ndc):
            dc = r_ * 8 + i
            waits = []
            if i < 2:
                if r_ < 2:
                    if war_pe:
                        waits.append(("pe", war_pe))
                    waits.append(("pool", c_pool0))
                    waits.append(("dve", c_dve0))
                elif len(casts) >= 2:
                    waits.append(("pool", casts[r_ - 2]))
            if i % 2 == 0:
                c_ld = p.dma("sync", stage[:, i * cols:(i + 1) * cols],
                             w_ap[dc * 128:(dc + 1) * 128, col0:col0 + cols], waits)
            else:
                c_ld2 = p.dma("pool", stage[:, i * cols:(i + 1) * cols],
                              w_ap[dc * 128:(dc + 1) * 128, col0:col0 + cols], waits,
                              inc="ld2")
        def cast(e, o=wbf[:, r_ * 8 * cols:(r_ * 8 + ndc) * cols],
                 i=stage[:, 0:ndc * cols]):
            return e.tensor_copy(o, i)
        w = [("ld", c_ld)]
        if c_ld2:
            w.append(("ld2", c_ld2))
        if war_pe:
            w.append(("pe", war_pe))
        casts.append(p.emit("pool", cast, w, "pool"))
    return casts if ret_all else casts[-1]


def bcast_row(p, G, dst, row_ap, onr, n, waits=()):
    c_mm = mm(p, G[:, 0:n], onr, row_ap, True, True, waits=list(waits), inc="pe")
    def f(e, o=dst, i=G[:, 0:n]):
        return e.activation(o, i, AF.Copy)
    return p.emit("act", f, [("pe", c_mm)], "act")


def act_evac(p, out_ap, psum_ap, waits, func=AF.Copy, bias=0.0, scale=1.0):
    if func == AF.Copy and not isinstance(bias, float):
        func = AF.Identity
    def f(e, o=out_ap, i=psum_ap, fu=func, b=bias, s=scale):
        return e.activation(o, i, fu, bias=b, scale=s)
    return p.emit("act", f, waits, "act")


def dve_op(p, fn, waits=(), inc="dve", eng="dve"):
    return p.emit(eng, fn, waits, inc)


def layernorm_phase(p, x_h, xt, sq, xn_tm, xn_fm, stt, psT, psT2, idb_sb,
                    qsum=None):
    c_prev_ts = None
    c_prev_tr_pe = None
    c_evac = None
    c_xn_list = []
    for rt in range(8):
        waits = [("dve", c_prev_ts)] if c_prev_ts else []
        c_x = p.dma("sync", xt[:, :], x_h[rt * 128:(rt + 1) * 128, :], waits)
        dve_op(p, lambda e, o=stt[:, 0:1], i=xt[:, :]: e.reduce_sum(o, i, axis=AX),
               [("ld", c_x)])
        c_sq = act_evac(p, sq[:, :], xt[:, :], [("ld", c_x)], func=AF.Square)
        dve_op(p, lambda e, o=stt[:, 1:2], i=sq[:, :]: e.reduce_sum(o, i, axis=AX),
               [("act", c_sq)])
        dve_op(p, lambda e, o=stt[:, 2:3], i=stt[:, 0:1]: e.tensor_scalar_mul(o, i, 1.0 / 4096.0))
        dve_op(p, lambda e, o=stt[:, 3:4], i=stt[:, 1:2]: e.tensor_scalar_mul(o, i, 1.0 / 4096.0))
        dve_op(p, lambda e, o=stt[:, 4:5], i=stt[:, 2:3]: e.tensor_mul(o, i, i))
        dve_op(p, lambda e, o=stt[:, 5:6], a=stt[:, 3:4], b=stt[:, 4:5]:
               e.tensor_sub(o, a, b))
        c_var = dve_op(p, lambda e, o=stt[:, 5:6], i=stt[:, 5:6]:
                       e.tensor_single_scalar(o, i, 1e-5, OP.add))
        c_std = act_evac(p, stt[:, 6:7], stt[:, 5:6], [("dve", c_var)], func=AF.Sqrt)
        dve_op(p, lambda e, o=stt[:, 7:8], i=stt[:, 6:7]: e.reciprocal(o, i),
               [("act", c_std)])
        waits = [("pe", c_prev_tr_pe)] if c_prev_tr_pe else []
        c_ts = dve_op(p, lambda e, o=xn_tm[:, :], i=xt[:, :], m=stt[:, 2:3],
                      r=stt[:, 7:8]: e.tensor_scalar(o, i, m, r, OP.subtract, OP.mult),
                      waits)
        c_xn_list.append(c_ts)
        c_prev_ts = c_ts
        if qsum is not None:
            mcol_sb, xsum_sb, psV = qsum
            c_qd_prev = None
            for ch in range(8):
                w = [("dve", c_ts)]
                if c_qd_prev:
                    w.append(("dve", c_qd_prev))
                c_qm = mm(p, psV[0:1, 0:512], mcol_sb[:, rt:rt + 1],
                          xn_tm[:, ch * 512:(ch + 1) * 512], True, True, w, inc="pe")
                if rt == 0:
                    c_qd_prev = dve_op(p, lambda e, o=xsum_sb[0:1, ch * 512:(ch + 1) * 512],
                                       i=psV[0:1, 0:512]: e.tensor_copy(o, i),
                                       [("pe", c_qm)])
                else:
                    c_qd_prev = dve_op(p, lambda e, o=xsum_sb[0:1, ch * 512:(ch + 1) * 512],
                                       i=psV[0:1, 0:512]: e.tensor_add(o, o, i),
                                       [("pe", c_qm)])
        for c in range(32):
            pst = psT if (c % 2 == 0) else psT2
            w = [("dve", c_ts)]
            if c_evac is not None:
                w.append(("act", c_evac if c < 2 else c_evac))
            c_tr = p.emit("pe", lambda e, o=pst[:, :], i=xn_tm[:, c * 128:(c + 1) * 128],
                          idn=idb_sb[:, :]: e.transpose(o, i, idn),
                          w + ([("act", c_evac)] if c_evac else []), "pe")
            c_prev_tr_pe = c_tr
            c_evac = p.emit("act", lambda e, o=xn_fm[:, c * 1024 + rt * 128:
                                                     c * 1024 + rt * 128 + 128],
                            i=pst[:, :]: e.copy(o, i), [("pe", c_tr)], "act")
    return c_evac, c_xn_list


def build_A():
    nc = bass.Bass("TRN2", target_bir_lowering=False)
    st = ExitStack()
    x_h = nc.dram_tensor("x", [R, D], F32, kind="ExternalInput").ap()
    wqkv_h = nc.dram_tensor("wqkv", [D, 3072], F32, kind="ExternalInput").ap()
    bqkv_h = nc.dram_tensor("bqkv", [3072], F32, kind="ExternalInput").ap()
    bwt_h = nc.dram_tensor("bwt", [D, D], F32, kind="ExternalInput").ap()
    bb_h = nc.dram_tensor("bb", [D], F32, kind="ExternalInput").ap()
    rwt_h = nc.dram_tensor("rwt", [D, 15], F32, kind="ExternalInput").ap()
    rb_h = nc.dram_tensor("rb", [15], F32, kind="ExternalInput").ap()
    at_h = nc.dram_tensor("at", [D, 128], F32, kind="ExternalInput").ap()
    bfl_h = nc.dram_tensor("bfl", [128, D], F32, kind="ExternalInput").ap()
    em_h = nc.dram_tensor("em", [15], I32, kind="ExternalInput").ap()
    idb_h = nc.dram_tensor("idb", [128, 128], BF16, kind="ExternalInput").ap()
    idf_h = nc.dram_tensor("idf", [128, 128], F32, kind="ExternalInput").ap()
    onr_h = nc.dram_tensor("onr", [1, 128], F32, kind="ExternalInput").ap()
    esel_h = nc.dram_tensor("esel", [8, 128], F32, kind="ExternalInput").ap()
    qT_h = nc.dram_tensor("qT", [1024, R], BF16, kind="ExternalOutput").ap()
    kT_h = nc.dram_tensor("kT", [1024, R], BF16, kind="ExternalOutput").ap()
    v_h = nc.dram_tensor("v", [R, 1024], BF16, kind="ExternalOutput").ap()
    ffn_h = nc.dram_tensor("ffn", [R, D], F32, kind="ExternalOutput").ap()

    xt = st.enter_context(nc.sbuf_tensor("xt", [128, D], F32))
    xn_tm = st.enter_context(nc.sbuf_tensor("xn_tm", [128, D], BF16))
    xn = st.enter_context(nc.sbuf_tensor("xn", [128, NDC * R], BF16))
    wst = st.enter_context(nc.sbuf_tensor("wst", [128, 8 * 512], F32))
    wst2 = st.enter_context(nc.sbuf_tensor("wst2", [128, 8 * 512], F32))
    sq = wst
    wbf = st.enter_context(nc.sbuf_tensor("wbf", [128, 32 * 512], BF16))
    stt = st.enter_context(nc.sbuf_tensor("stt", [128, 8], F32))
    bq = st.enter_context(nc.sbuf_tensor("bq", [128, 16], F32))
    vbb = st.enter_context(nc.sbuf_tensor("vbb", [128, 1024], F32))
    rbrow = st.enter_context(nc.sbuf_tensor("rbrow", [1, 15], F32))
    rbb = st.enter_context(nc.sbuf_tensor("rbb", [128, 15], F32))
    emi = st.enter_context(nc.sbuf_tensor("emi", [1, 15], I32))
    emf = st.enter_context(nc.sbuf_tensor("emf", [1, 15], F32))
    mapb = st.enter_context(nc.sbuf_tensor("mapb", [128, 15], F32))
    idb_sb = st.enter_context(nc.sbuf_tensor("idbs", [128, 128], BF16))
    idf_sb = st.enter_context(nc.sbuf_tensor("idfs", [128, 128], F32))
    onr_sb = st.enter_context(nc.sbuf_tensor("onrs", [1, 128], F32))
    esel_sb = st.enter_context(nc.sbuf_tensor("esels", [8, 128], F32))
    at_bf = st.enter_context(nc.sbuf_tensor("atbf", [128, D], BF16))
    bfl_bf = st.enter_context(nc.sbuf_tensor("bflbf", [128, D], BF16))
    rwt_bf = st.enter_context(nc.sbuf_tensor("rwtbf", [128, 32 * 15], BF16))
    lgb = st.enter_context(nc.sbuf_tensor("lgb", [128, 8 * 15], F32))
    prb = st.enter_context(nc.sbuf_tensor("prb", [128, 15], F32))
    tmp15 = st.enter_context(nc.sbuf_tensor("tmp15", [128, 15], F32))
    ranks = st.enter_context(nc.sbuf_tensor("ranks", [128, 15], F32))
    msk15 = st.enter_context(nc.sbuf_tensor("msk15", [128, 15], F32))
    wgt = st.enter_context(nc.sbuf_tensor("wgt", [128, 15], F32))
    me = st.enter_context(nc.sbuf_tensor("me", [128, 15], F32))
    wsb = st.enter_context(nc.sbuf_tensor("wsb", [128, 8], F32))
    wT = st.enter_context(nc.sbuf_tensor("wT", [8, 1024], F32))
    wex = st.enter_context(nc.sbuf_tensor("wex", [128, 1024], F32))
    h_sb = st.enter_context(nc.sbuf_tensor("hsb", [128, 1024], F32))
    hw = st.enter_context(nc.sbuf_tensor("hw", [128, 1024], BF16))
    stg = [st.enter_context(nc.sbuf_tensor(f"stg{i}", [128, 1024], BF16)) for i in range(2)]
    stgf = [st.enter_context(nc.sbuf_tensor(f"stgf{i}", [128, 512], F32)) for i in range(2)]
    stgv = [st.enter_context(nc.sbuf_tensor(f"stgv{i}", [128, 512], BF16)) for i in range(2)]
    bbrow = st.enter_context(nc.sbuf_tensor("bbrow", [1, 512], F32))
    bcb = st.enter_context(nc.sbuf_tensor("bcb", [128, 512], F32))

    G = [st.enter_context(nc.psum_tensor(f"G{i}", [128, 512], F32)) for i in range(4)]
    psT = st.enter_context(nc.psum_tensor("psT", [128, 128], BF16))
    psT2 = st.enter_context(nc.psum_tensor("psT2", [128, 128], BF16))
    psTf = st.enter_context(nc.psum_tensor("psTf", [128, 128], F32))

    p = PB(nc, st)

    p.dma("sync", idb_sb[:, :], idb_h[:, :])
    p.dma("sync", idf_sb[:, :], idf_h[:, :])
    p.dma("sync", onr_sb[:, :], onr_h[:, :])
    p.dma("sync", esel_sb[:, :], esel_h[:, :])
    p.dma("sync", emi[:, :], em_h[:])
    p.dma("sync", rbrow[:, :], rb_h[:])
    c_c = None
    for of in range(16):
        c_c = p.dma("sync", bq[:, of:of + 1], bqkv_h[of * 128:(of + 1) * 128])
    c_emf = dve_op(p, lambda e, o=emf[:, :], i=emi[:, :]: e.tensor_copy(o, i),
                   [("ld", c_c)])
    c_mapb = bcast_row(p, G[2], mapb[:, :], emf[:, :], onr_sb[:, :], 15,
                       [("dve", c_emf), ("ld", c_c)])
    c_rbb = bcast_row(p, G[2], rbb[:, :], rbrow[:, :], onr_sb[:, :], 15,
                      [("ld", c_c), ("act", c_mapb)])
    c_vb0 = p.dma("sync", bbrow[0:1, :], bqkv_h[2048:2560])
    c_vbb0 = bcast_row(p, G[2], vbb[:, 0:512], bbrow[0:1, :], onr_sb[:, :], 512,
                       [("act", c_rbb), ("ld", c_vb0)])
    c_vb1 = p.dma("sync", bbrow[0:1, :], bqkv_h[2560:3072], [("pe", p.cnt["pe"])])
    c_vbb = bcast_row(p, G[2], vbb[:, 512:1024], bbrow[0:1, :], onr_sb[:, :], 512,
                      [("act", c_vbb0), ("ld", c_vb1)])

    c_xn_done, _ = layernorm_phase(p, x_h, xt, sq, xn_tm, xn_fm=xn, stt=stt,
                                   psT=psT, psT2=psT2, idb_sb=idb_sb)

    c_war_pe = None
    c_sta_prev = {}
    n_store = 0
    for og in range(4):
        c_cast = wload(p, wqkv_h, NDC, og * 512, 512, wst, wst2, wbf, c_war_pe, ret_all=True)
        for of_l in range(4):
            of = og * 4 + of_l
            for rh in range(2):
                bank = G[rh]; bi = rh
                c_mm = None
                for dc in range(NDC):
                    w = []
                    if dc % 8 == 0:
                        w.append(("pool", c_cast[dc // 8]))
                    if dc == 0:
                        w.append(("act", c_xn_done))
                        if bi in c_sta_prev:
                            w.append(("act", c_sta_prev[bi]))
                    c_mm = mm(p, bank[:, :], wbf[:, dc * 512 + of_l * 128:dc * 512 + of_l * 128 + 128],
                              xn[:, dc * 1024 + rh * 512:dc * 1024 + rh * 512 + 512],
                              dc == 0, dc == NDC - 1, w, inc="pe" if dc == NDC - 1 else None)
                s = stg[of % 2]
                w = [("pe", c_mm)]
                if n_store >= 2:
                    w.append(("sta", (n_store - 1) * 16))
                c_ev = act_evac(p, s[:, rh * 512:(rh + 1) * 512], bank[:, :], w,
                                bias=bq[:, of:of + 1])
                c_sta_prev[bi] = c_ev
            dst = qT_h if of < 8 else kT_h
            p.dma("act", dst[(of % 8) * 128:(of % 8) * 128 + 128, :], stg[of % 2][:, :],
                  [], inc="sta")
            n_store += 1
        c_war_pe = p.cnt["pe"]

    n_std = 0
    c_std_prev = {}
    for ogv in range(2):
        c_cast = wload(p, wqkv_h, NDC, 2048 + ogv * 512, 512, wst, wst2, wbf, c_war_pe, ret_all=True)
        for rs in range(8):
            bank = G[rs % 2]; bi = rs % 2
            c_mm = None
            for dc in range(NDC):
                w = []
                if dc % 8 == 0:
                    w.append(("pool", c_cast[dc // 8]))
                if dc == 0:
                    w.append(("act", c_vbb))
                    if bi in c_sta_prev:
                        w.append(("act", c_sta_prev.pop(bi)))
                    if bi in c_std_prev:
                        w.append(("dve", c_std_prev[bi]))
                c_mm = mm(p, bank[:, :], xn[:, dc * 1024 + rs * 128:dc * 1024 + rs * 128 + 128],
                          wbf[:, dc * 512:(dc + 1) * 512],
                          dc == 0, dc == NDC - 1, w, inc="pe" if dc == NDC - 1 else None)
            sv = stgv[rs % 2]
            w = [("pe", c_mm)]
            if n_std >= 2:
                w.append(("std", (n_std - 1) * 16))
            c_ev = dve_op(p, lambda e, o=sv[:, :], a=bank[:, :], b=vbb[:, ogv * 512:(ogv + 1) * 512]:
                          e.tensor_add(o, a, b), w)
            c_std_prev[bi] = c_ev
            p.dma("act", v_h[rs * 128:(rs + 1) * 128, ogv * 512:(ogv + 1) * 512], sv[:, :],
                  [("dve", c_ev)], inc="std")
            n_std += 1
        c_war_pe = p.cnt["pe"]

    c_cast = wload(p, rwt_h, NDC, 0, 15, wst, wst2, wbf, c_war_pe)
    c_rw = dve_op(p, lambda e, o=rwt_bf[:, :], i=wbf[:, 0:480]: e.tensor_copy(o, i),
                  [("pool", c_cast)], eng="pool", inc="pool")
    c_cast = wload(p, at_h, NDC, 0, 128, wst, wst2, wbf, None)
    c_at = dve_op(p, lambda e, o=at_bf[:, :], i=wbf[:, 0:4096]: e.tensor_copy(o, i),
                  [("pool", c_cast)], eng="pool", inc="pool")
    c_ld_bfl = None
    for i in range(4):
        c_ld_bfl = p.dma("sync", wst[:, i * 1024:(i + 1) * 1024],
                         bfl_h[:, i * 1024:(i + 1) * 1024], [("pool", c_at)])
    c_bfl = dve_op(p, lambda e, o=bfl_bf[:, :], i=wst[:, 0:4096]: e.tensor_copy(o, i),
                   [("ld", c_ld_bfl)], eng="pool", inc="pool")

    c_lg = {}
    for rs in range(8):
        bank = G[2]
        c_mm = None
        for dc in range(NDC):
            w = []
            if dc == 0:
                w = [("pool", c_rw), ("act", c_xn_done)]
                if rs > 0:
                    w.append(("dve", c_lg[rs - 1]))
            c_mm = mm(p, bank[:, 0:15], xn[:, dc * 1024 + rs * 128:dc * 1024 + rs * 128 + 128],
                      rwt_bf[:, dc * 15:(dc + 1) * 15],
                      dc == 0, dc == NDC - 1, w, inc="pe" if dc == NDC - 1 else None)
        c_lg[rs] = dve_op(p, lambda e, o=lgb[:, rs * 15:(rs + 1) * 15], a=bank[:, 0:15],
                          b=rbb[:, :]: e.tensor_add(o, a, b), [("pe", c_mm)])

    c_h = None
    for rh in range(2):
        bank = G[3]
        c_mm = None
        for dc in range(NDC):
            w = [("pool", c_at)] if dc == 0 else []
            if dc == 0 and c_h is not None:
                w.append(("act", c_h))
            c_mm = mm(p, bank[:, :], at_bf[:, dc * 128:(dc + 1) * 128],
                      xn[:, dc * 1024 + rh * 512:dc * 1024 + rh * 512 + 512],
                      dc == 0, dc == NDC - 1, w, inc="pe" if dc == NDC - 1 else None)
        c_h = act_evac(p, h_sb[:, rh * 512:(rh + 1) * 512], bank[:, :], [("pe", c_mm)])

    c_wT = None
    c_tr_prev = None
    for rs in range(8):
        lg = lgb[:, rs * 15:(rs + 1) * 15]
        dve_op(p, lambda e, o=stt[:, 0:1], i=lg: e.reduce_max(o, i, axis=AX),
               [("dve", c_lg[7])] + ([("pe", c_tr_prev)] if c_tr_prev else []))
        c_nm = dve_op(p, lambda e, o=stt[:, 1:2], i=stt[:, 0:1]: e.tensor_scalar_mul(o, i, -1.0))
        c_eu = act_evac(p, prb[:, :], lg, [("dve", c_nm)], func=AF.Exp, bias=stt[:, 1:2])
        dve_op(p, lambda e, o=stt[:, 2:3], i=prb[:, :]: e.reduce_sum(o, i, axis=AX),
               [("act", c_eu)])
        dve_op(p, lambda e, o=stt[:, 3:4], i=stt[:, 2:3]: e.reciprocal(o, i))
        dve_op(p, lambda e, o=prb[:, :], i=prb[:, :], s=stt[:, 3:4]: e.tensor_scalar_mul(o, i, s))
        for n in range(15):
            dve_op(p, lambda e, o=tmp15[:, :], i=lg, s=lg[:, n:n + 1]:
                   e.tensor_single_scalar(o, i, s, OP.is_gt))
            dve_op(p, lambda e, o=ranks[:, n:n + 1], i=tmp15[:, :]: e.reduce_sum(o, i, axis=AX))
        dve_op(p, lambda e, o=msk15[:, :], i=ranks[:, :]:
               e.tensor_single_scalar(o, i, 7.5, OP.is_lt))
        dve_op(p, lambda e, o=wgt[:, :], a=prb[:, :], b=msk15[:, :]: e.tensor_mul(o, a, b))
        c_w = None
        for ex in range(8):
            dve_op(p, lambda e, o=me[:, :], i=mapb[:, :], v=float(ex):
                   e.tensor_single_scalar(o, i, v, OP.is_equal), [("act", c_mapb)])
            dve_op(p, lambda e, o=me[:, :], a=wgt[:, :], b=me[:, :]: e.tensor_mul(o, a, b))
            c_w = dve_op(p, lambda e, o=wsb[:, ex:ex + 1], i=me[:, :]: e.reduce_sum(o, i, axis=AX))
        w = [("dve", c_w)]
        if c_wT is not None:
            w.append(("act", c_wT))
        c_tr = p.emit("pe", lambda e, o=psTf[0:8, :], i=wsb[:, :], idn=idf_sb[:, :]:
                      e.transpose(o, i, idn), w, "pe")
        c_tr_prev = c_tr
        c_wT = p.emit("act", lambda e, o=wT[0:8, rs * 128:(rs + 1) * 128], i=psTf[0:8, :]:
                      e.copy(o, i), [("pe", c_tr)], "act")
    c_wex = None
    for hh in range(2):
        c_mm = mm(p, G[2][:, :], esel_sb[:, :], wT[0:8, hh * 512:(hh + 1) * 512],
                  True, True, [("act", c_wT)] + ([("act", c_wex)] if c_wex else []), inc="pe")
        c_wex = act_evac(p, wex[:, hh * 512:(hh + 1) * 512], G[2][:, :], [("pe", c_mm)])
    c_hw = dve_op(p, lambda e, o=hw[:, :], a=h_sb[:, :], b=wex[:, :]: e.tensor_mul(o, a, b),
                  [("act", c_wex), ("act", c_h)])

    c_war_pe = p.cnt["pe"]
    c_bcb = None
    c_std_prev = {}
    for oc in range(8):
        c_cast = wload(p, bwt_h, NDC, oc * 512, 512, wst, wst2, wbf, c_war_pe, ret_all=True)
        c_bb = p.dma("sync", bbrow[0:1, :], bb_h[oc * 512:(oc + 1) * 512],
                     [("act", c_bcb)] if c_bcb else [])
        c_bcb = bcast_row(p, G[2], bcb[:, :], bbrow[0:1, :],
                          onr_sb[:, :], 512,
                          [("ld", c_bb), ("dve", p.cnt["dve"]), ("act", p.cnt["act"])])
        for rs in range(8):
            bank = G[rs % 2]; bi = rs % 2
            c_mm = None
            for dc in range(NDC):
                w = []
                if dc % 8 == 0:
                    w.append(("pool", c_cast[dc // 8]))
                if dc == 0:
                    w += [("dve", c_hw), ("pool", c_bfl)]
                    if bi in c_std_prev:
                        w.append(("dve", c_std_prev[bi]))
                c_mm = mm(p, bank[:, :], xn[:, dc * 1024 + rs * 128:dc * 1024 + rs * 128 + 128],
                          wbf[:, dc * 512:(dc + 1) * 512], dc == 0, False, w)
            c_mm = mm(p, bank[:, :], hw[:, rs * 128:(rs + 1) * 128],
                      bfl_bf[:, oc * 512:(oc + 1) * 512], False, True, [], inc="pe")
            sf = stgf[rs % 2]
            w = [("pe", c_mm), ("act", c_bcb)]
            if n_std >= 2:
                w.append(("std", (n_std - 1) * 16))
            c_ev = dve_op(p, lambda e, o=sf[:, :], a=bank[:, :], b=bcb[:, :]:
                          e.tensor_add(o, a, b), w)
            c_std_prev[bi] = c_ev
            p.dma("act", ffn_h[rs * 128:(rs + 1) * 128, oc * 512:(oc + 1) * 512], sf[:, :],
                  [("dve", c_ev)], inc="std")
            n_std += 1
        c_war_pe = p.cnt["pe"]

    p.wait_only("sync", [("sta", p.cnt["sta"]), ("std", p.cnt["std"])])
    p.finalize()
    st.close()
    return nc


def build_B():
    nc = bass.Bass("TRN2", target_bir_lowering=False)
    st = ExitStack()
    x_h = nc.dram_tensor("x", [R, D], F32, kind="ExternalInput").ap()
    qT_h = nc.dram_tensor("qT", [1024, R], BF16, kind="ExternalInput").ap()
    KT_h = nc.dram_tensor("KT", [1024, 4096], BF16, kind="ExternalInput").ap()
    V_h = nc.dram_tensor("V", [4096, 1024], BF16, kind="ExternalInput").ap()
    wo_h = nc.dram_tensor("wo", [1024, D], F32, kind="ExternalInput").ap()
    bo_h = nc.dram_tensor("bo", [D], F32, kind="ExternalInput").ap()
    ffn_h = nc.dram_tensor("ffn", [R, D], F32, kind="ExternalInput").ap()
    onr_h = nc.dram_tensor("onr", [1, 128], F32, kind="ExternalInput").ap()
    onb_h = nc.dram_tensor("onb", [128, 1], BF16, kind="ExternalInput").ap()
    x1_h = nc.dram_tensor("x1", [R, D], F32, kind="ExternalOutput").ap()

    onr_sb = st.enter_context(nc.sbuf_tensor("onrs", [1, 128], F32))
    onb_sb = st.enter_context(nc.sbuf_tensor("onbs", [128, 1], BF16))
    O_sb = st.enter_context(nc.sbuf_tensor("Osb", [128, 8 * 1024], BF16))
    smal = st.enter_context(nc.sbuf_tensor("smal", [1, 1024], F32))
    sc2 = st.enter_context(nc.sbuf_tensor("sc2", [1, 8], F32))
    negm = st.enter_context(nc.sbuf_tensor("negm", [128, 1], F32))
    recipb = st.enter_context(nc.sbuf_tensor("recipb", [128, 512], F32))
    rcp_row = st.enter_context(nc.sbuf_tensor("rcprow", [1, 512], F32))

    G = [st.enter_context(nc.psum_tensor(f"G{i}", [128, 512], F32)) for i in range(4)]
    psV = st.enter_context(nc.psum_tensor("psV", [128, 512], F32))

    p = PB(nc, st)
    p.dma("sync", onr_sb[:, :], onr_h[:, :])
    p.dma("sync", onb_sb[:, :], onb_h[:, :])

    attn_st = ExitStack()
    q_sb = attn_st.enter_context(nc.sbuf_tensor("qsb", [128, 8 * 1024], BF16))
    K_sb = attn_st.enter_context(nc.sbuf_tensor("Ksb", [128, 8 * 4096], BF16))
    PT = attn_st.enter_context(nc.sbuf_tensor("PT", [128, 32 * 512], BF16))
    Vb = [attn_st.enter_context(nc.sbuf_tensor(f"Vb{i}", [128, 32 * 128], BF16)) for i in range(2)]
    sqb = attn_st.enter_context(nc.sbuf_tensor("sqb", [128, 4096], BF16))

    c_q = None
    for dc in range(8):
        c_q = p.dma("sync", q_sb[:, dc * 1024:(dc + 1) * 1024], qT_h[dc * 128:(dc + 1) * 128, :])
    c_k = None
    for dc in range(8):
        c_k = p.dma("sync", K_sb[:, dc * 4096:(dc + 1) * 4096], KT_h[dc * 128:(dc + 1) * 128, :])

    c_norm_evac = None
    c_norm_mm = None
    for half in range(2):
        c_mm = None
        for dc in range(8):
            w = [("ld", c_q)]
            if c_norm_mm is not None:
                w.append(("pe", c_norm_mm))
            c_s = act_evac(p, sqb[:, dc * 512:(dc + 1) * 512],
                           q_sb[:, dc * 1024 + half * 512:dc * 1024 + half * 512 + 512],
                           w, func=AF.Square)
            w = [("act", c_s)]
            if dc == 0 and c_norm_evac is not None:
                w.append(("dve", c_norm_evac))
            c_mm = mm(p, psV[0:1, :], onb_sb[:, :], sqb[:, dc * 512:(dc + 1) * 512],
                      dc == 0, dc == 7, w, inc="pe" if dc == 7 else None)
        c_norm_mm = c_mm
        c_norm_evac = dve_op(p, lambda e, o=smal[0:1, half * 512:(half + 1) * 512],
                             i=psV[0:1, :]: e.tensor_copy(o, i), [("pe", c_mm)])
    dve_op(p, lambda e, o=sc2[0:1, 0:1], i=smal[0:1, :]: e.reduce_max(o, i, axis=AX),
           [("dve", c_norm_evac)])
    c_kn = None
    for ch in range(8):
        c_mm = None
        for dc in range(8):
            c_s = act_evac(p, sqb[:, dc * 512:(dc + 1) * 512],
                           K_sb[:, dc * 4096 + ch * 512:dc * 4096 + ch * 512 + 512],
                           [("ld", c_k), ("pe", c_norm_mm)], func=AF.Square)
            w = [("act", c_s)]
            if dc == 0:
                w.append(("dve", c_norm_evac if c_kn is None else c_kn))
            c_mm = mm(p, psV[0:1, :], onb_sb[:, :], sqb[:, dc * 512:(dc + 1) * 512],
                      dc == 0, dc == 7, w, inc="pe" if dc == 7 else None)
        c_norm_mm = c_mm
        c_kn = dve_op(p, lambda e, o=smal[0:1, 0:512], i=psV[0:1, :]: e.tensor_copy(o, i),
                      [("pe", c_mm)])
        if ch == 0:
            c_kn = dve_op(p, lambda e, o=sc2[0:1, 1:2], i=smal[0:1, 0:512]:
                          e.reduce_max(o, i, axis=AX))
        else:
            dve_op(p, lambda e, o=sc2[0:1, 2:3], i=smal[0:1, 0:512]: e.reduce_max(o, i, axis=AX))
            c_kn = dve_op(p, lambda e, o=sc2[0:1, 1:2], a=sc2[0:1, 1:2], b=sc2[0:1, 2:3]:
                          e.tensor_max(o, a, b))
    c_m2 = dve_op(p, lambda e, o=sc2[0:1, 3:4], a=sc2[0:1, 0:1], b=sc2[0:1, 1:2]:
                  e.tensor_mul(o, a, b), [("dve", c_kn)])
    c_sqr = act_evac(p, sc2[0:1, 4:5], sc2[0:1, 3:4], [("dve", c_m2)], func=AF.Sqrt)
    c_neg = dve_op(p, lambda e, o=sc2[0:1, 5:6], i=sc2[0:1, 4:5]:
                   e.tensor_scalar_mul(o, i, -1.0 / 32.0), [("act", c_sqr)])
    c_mmn = mm(p, G[3][:, 0:1], onr_sb[:, :], sc2[0:1, 5:6], True, True,
               [("dve", c_neg)], inc="pe")
    c_negb = act_evac(p, negm[:, :], G[3][:, 0:1], [("pe", c_mmn)])

    n_std = 0
    c_o_ev = None
    c_exp_prev = {}
    c_pv_par = {0: None, 1: None}
    for qc in range(2):
        c_exp = {}
        for kt in range(32):
            bank = G[kt % 2]
            w = [("ld", c_k), ("ld", c_q)]
            if (qc, kt) != (0, 0):
                prev = c_exp_prev.get(kt % 2)
                if prev:
                    w.append(("act", prev))
            if qc == 1:
                w.append(("pe", c_pv_par[0] or 0))
                w.append(("pe", c_pv_par[1] or 0))
            c_mm = None
            for dc in range(8):
                c_mm = mm(p, bank[:, :], K_sb[:, dc * 4096 + kt * 128:dc * 4096 + kt * 128 + 128],
                          q_sb[:, dc * 1024 + qc * 512:dc * 1024 + qc * 512 + 512],
                          dc == 0, dc == 7, w if dc == 0 else [],
                          inc="pe" if dc == 7 else None)
            c_exp[kt] = act_evac(p, PT[:, kt * 512:(kt + 1) * 512], bank[:, :],
                                 [("pe", c_mm), ("act", c_negb)],
                                 func=AF.Exp, bias=negm[:, :], scale=1.0 / 32.0)
            c_exp_prev[kt % 2] = c_exp[kt]
        c_mm = None
        for kt in range(32):
            w = [("act", c_exp[kt])]
            if kt == 0 and c_o_ev is not None:
                w.append(("dve", c_o_ev))
            c_mm = mm(p, psV[0:1, :], onb_sb[:, :], PT[:, kt * 512:(kt + 1) * 512],
                      kt == 0, kt == 31, w, inc="pe" if kt == 31 else None)
        c_rcp = dve_op(p, lambda e, o=rcp_row[0:1, :], i=psV[0:1, :]: e.reciprocal(o, i),
                       [("pe", c_mm)])
        c_mm = mm(p, G[3][:, :], onr_sb[:, :], rcp_row[0:1, :], True, True,
                  [("dve", c_rcp), ("act", c_negb)], inc="pe")
        w = [("pe", c_mm)]
        if c_o_ev is not None:
            w.append(("dve", c_o_ev))
        c_rb = act_evac(p, recipb[:, :], G[3][:, :], w)
        for vc in range(8):
            vb = Vb[vc % 2]
            c_v = None
            for kt in range(32):
                wv = []
                if kt == 0 and c_pv_par[vc % 2] is not None:
                    wv = [("pe", c_pv_par[vc % 2])]
                c_v = p.dma("sync", vb[:, kt * 128:(kt + 1) * 128],
                            V_h[kt * 128:(kt + 1) * 128, vc * 128:(vc + 1) * 128], wv)
            c_mm = None
            for kt in range(32):
                w = [("ld", c_v), ("act", c_exp[31])]
                if kt == 0 and c_o_ev is not None:
                    w.append(("dve", c_o_ev))
                c_mm = mm(p, G[2][:, :], vb[:, kt * 128:(kt + 1) * 128],
                          PT[:, kt * 512:(kt + 1) * 512], kt == 0, kt == 31,
                          w if kt == 0 else [], inc="pe" if kt == 31 else None)
            c_pv_par[vc % 2] = c_mm
            c_o_ev = dve_op(p, lambda e, o=O_sb[:, vc * 1024 + qc * 512:vc * 1024 + qc * 512 + 512],
                            a=G[2][:, :], b=recipb[:, :]: e.tensor_mul(o, a, b),
                            [("pe", c_mm), ("act", c_rb)])

    attn_done_pe = p.cnt["pe"]
    attn_done_dve = p.cnt["dve"]
    attn_st.close()

    wst = st.enter_context(nc.sbuf_tensor("wst", [128, 8 * 512], F32))
    wbf = st.enter_context(nc.sbuf_tensor("wbf", [128, 8 * 512], BF16))
    xcol = st.enter_context(nc.sbuf_tensor("xcol", [128, 8 * 512], F32))
    fcol = st.enter_context(nc.sbuf_tensor("fcol", [128, 8 * 512], F32))
    borow = st.enter_context(nc.sbuf_tensor("borow", [1, 512], F32))
    bob = st.enter_context(nc.sbuf_tensor("bob", [128, 512], F32))
    stgf = [st.enter_context(nc.sbuf_tensor(f"stgf{i}", [128, 512], F32)) for i in range(2)]

    p.wait_only("sync", [("dve", attn_done_dve)])
    c_res_prev = None
    c_std_prev = {}
    c_bob = None
    for oc in range(8):
        c_cast = wload(p, wo_h, 8, oc * 512, 512, wst, wst, wbf,
                       attn_done_pe if oc == 0 else p.cnt["pe"])
        c_xc = fc = None
        for rs in range(8):
            w = [("dve", c_res_prev)] if (oc > 0 and rs == 0 and c_res_prev) else []
            c_xc = p.dma("sync", xcol[:, rs * 512:(rs + 1) * 512],
                         x_h[rs * 128:(rs + 1) * 128, oc * 512:(oc + 1) * 512], w)
        for rs in range(8):
            fc = p.dma("sync", fcol[:, rs * 512:(rs + 1) * 512],
                       ffn_h[rs * 128:(rs + 1) * 128, oc * 512:(oc + 1) * 512])
        c_bo = p.dma("sync", borow[0:1, :], bo_h[oc * 512:(oc + 1) * 512])
        c_bob = bcast_row(p, G[3], bob[:, :], borow[0:1, :], onr_sb[:, :], 512,
                          [("ld", c_bo), ("dve", p.cnt["dve"])] +
                          ([("act", c_bob)] if c_bob else []))
        for rs in range(8):
            qc = rs // 4
            bank = G[rs % 2]; bi = rs % 2
            c_mm = None
            for dc in range(8):
                w = []
                if dc == 0:
                    w = [("pool", c_cast), ("dve", attn_done_dve)]
                    if bi in c_std_prev:
                        w.append(("dve", c_std_prev[bi]))
                c_mm = mm(p, bank[:, :],
                          O_sb[:, dc * 1024 + qc * 512 + (rs % 4) * 128:
                               dc * 1024 + qc * 512 + (rs % 4) * 128 + 128],
                          wbf[:, dc * 512:(dc + 1) * 512], dc == 0, dc == 7,
                          w, inc="pe" if dc == 7 else None)
            sf = stgf[rs % 2]
            w = [("pe", c_mm), ("act", c_bob), ("ld", c_xc), ("ld", fc)]
            if n_std >= 2:
                w.append(("std", (n_std - 1) * 16))
            dve_op(p, lambda e, o=sf[:, :], a=bank[:, :], b=bob[:, :]: e.tensor_add(o, a, b), w)
            dve_op(p, lambda e, o=sf[:, :], a=sf[:, :], b=xcol[:, rs * 512:(rs + 1) * 512]:
                   e.tensor_add(o, a, b))
            c_ev = dve_op(p, lambda e, o=sf[:, :], a=sf[:, :], b=fcol[:, rs * 512:(rs + 1) * 512]:
                          e.tensor_add(o, a, b))
            c_std_prev[bi] = c_ev
            c_res_prev = c_ev
            p.dma("act", x1_h[rs * 128:(rs + 1) * 128, oc * 512:(oc + 1) * 512], sf[:, :],
                  [("dve", c_ev)], inc="std")
            n_std += 1

    p.wait_only("sync", [("std", p.cnt["std"])])
    p.finalize()
    st.close()
    return nc


def build_C():
    nc = bass.Bass("TRN2", target_bir_lowering=False)
    st = ExitStack()
    x_h = nc.dram_tensor("x", [R, D], F32, kind="ExternalInput").ap()
    wqkv_h = nc.dram_tensor("wqkv", [D, 6144], F32, kind="ExternalInput").ap()
    bqkv_h = nc.dram_tensor("bqkv", [6144], F32, kind="ExternalInput").ap()
    bwt_h = nc.dram_tensor("bwt", [D, D], F32, kind="ExternalInput").ap()
    bb_h = nc.dram_tensor("bb", [D], F32, kind="ExternalInput").ap()
    at_h = nc.dram_tensor("at", [D, 128], F32, kind="ExternalInput").ap()
    ropec_h = nc.dram_tensor("ropec", [128, 1024], F32, kind="ExternalInput").ap()
    ropes_h = nc.dram_tensor("ropes", [128, 1024], F32, kind="ExternalInput").ap()
    pswap_h = nc.dram_tensor("pswap", [128, 128], F32, kind="ExternalInput").ap()
    mcol_h = nc.dram_tensor("mcol", [R], BF16, kind="ExternalInput").ap()
    idb_h = nc.dram_tensor("idb", [128, 128], BF16, kind="ExternalInput").ap()
    onr_h = nc.dram_tensor("onr", [1, 128], F32, kind="ExternalInput").ap()
    qT_h = nc.dram_tensor("qT", [4096, R], BF16, kind="ExternalOutput").ap()
    kT_h = nc.dram_tensor("kT", [1024, R], BF16, kind="ExternalOutput").ap()
    v_h = nc.dram_tensor("v", [R, 1024], BF16, kind="ExternalOutput").ap()
    base_h = nc.dram_tensor("base", [R, D], F32, kind="ExternalOutput").ap()
    h1_h = nc.dram_tensor("h1", [128, R], BF16, kind="ExternalOutput").ap()
    qsum_h = nc.dram_tensor("qsum", [1, D], F32, kind="ExternalOutput").ap()

    xt = st.enter_context(nc.sbuf_tensor("xt", [128, D], F32))
    fbuf = st.enter_context(nc.sbuf_tensor("fbuf", [128, 2048], F32))
    xn_tm = st.enter_context(nc.sbuf_tensor("xn_tm", [128, D], BF16))
    xn = st.enter_context(nc.sbuf_tensor("xn", [128, NDC * R], BF16))
    wst = st.enter_context(nc.sbuf_tensor("wst", [128, 8 * 512], F32))
    wst2 = st.enter_context(nc.sbuf_tensor("wst2", [128, 8 * 512], F32))
    wbf = st.enter_context(nc.sbuf_tensor("wbf", [128, 32 * 512], BF16))
    stt = st.enter_context(nc.sbuf_tensor("stt", [128, 8], F32))
    bq = st.enter_context(nc.sbuf_tensor("bq", [128, 40], F32))
    vbb = st.enter_context(nc.sbuf_tensor("vbb", [128, 1024], F32))
    idb_sb = st.enter_context(nc.sbuf_tensor("idbs", [128, 128], BF16))
    onr_sb = st.enter_context(nc.sbuf_tensor("onrs", [1, 128], F32))
    pswap_sb = st.enter_context(nc.sbuf_tensor("pswaps", [128, 128], F32))
    ropec_sb = st.enter_context(nc.sbuf_tensor("ropecs", [128, 1024], F32))
    ropes_sb = st.enter_context(nc.sbuf_tensor("ropess", [128, 1024], F32))
    mcol_sb = st.enter_context(nc.sbuf_tensor("mcols", [128, 8], BF16))
    xsum = st.enter_context(nc.sbuf_tensor("xsum", [1, D], F32))
    stg = [st.enter_context(nc.sbuf_tensor(f"stg{i}", [128, 1024], BF16)) for i in range(2)]
    stgf = [st.enter_context(nc.sbuf_tensor(f"stgf{i}", [128, 512], F32)) for i in range(2)]
    stgv = [stg[0][:, 0:512], stg[1][:, 0:512]]
    bbrow = st.enter_context(nc.sbuf_tensor("bbrow", [1, 512], F32))
    bcb = st.enter_context(nc.sbuf_tensor("bcb", [128, 512], F32))

    G = [st.enter_context(nc.psum_tensor(f"G{i}", [128, 512], F32)) for i in range(4)]
    psT = st.enter_context(nc.psum_tensor("psT", [128, 128], BF16))
    psT2 = st.enter_context(nc.psum_tensor("psT2", [128, 128], BF16))
    psV = st.enter_context(nc.psum_tensor("psV", [128, 512], F32))

    p = PB(nc, st)
    p.dma("sync", idb_sb[:, :], idb_h[:, :])
    p.dma("sync", onr_sb[:, :], onr_h[:, :])
    p.dma("sync", pswap_sb[:, :], pswap_h[:, :])
    p.dma("sync", ropec_sb[:, :], ropec_h[:, :])
    p.dma("sync", ropes_sb[:, :], ropes_h[:, :])
    c_c = None
    for rt in range(8):
        p.dma("sync", mcol_sb[:, rt:rt + 1], mcol_h[rt * 128:(rt + 1) * 128])
    for of in range(40):
        c_c = p.dma("sync", bq[:, of:of + 1], bqkv_h[of * 128:(of + 1) * 128])
    c_ropetab = c_c
    c_vb0 = p.dma("sync", bbrow[0:1, :], bqkv_h[5120:5632])
    c_vbb0 = bcast_row(p, G[2], vbb[:, 0:512], bbrow[0:1, :], onr_sb[:, :], 512,
                       [("ld", c_vb0)])
    c_vb1 = p.dma("sync", bbrow[0:1, :], bqkv_h[5632:6144], [("pe", p.cnt["pe"])])
    c_vbb = bcast_row(p, G[2], vbb[:, 512:1024], bbrow[0:1, :], onr_sb[:, :], 512,
                      [("act", c_vbb0), ("ld", c_vb1)])

    c_xn_done, _ = layernorm_phase(p, x_h, xt, wst, xn_tm, xn_fm=xn, stt=stt,
                                   psT=psT, psT2=psT2, idb_sb=idb_sb,
                                   qsum=(mcol_sb, xsum, psV))
    p.dma("act", qsum_h[0:1, :], xsum[0:1, :], [("dve", p.cnt["dve"])], inc="std")

    c_war_pe = None
    n_store = 0
    c_rope_prev = None
    c_t2 = None
    for og in range(10):
        c_cast = wload(p, wqkv_h, NDC, og * 512, 512, wst, wst2, wbf, c_war_pe, ret_all=True)
        for of_l in range(4):
            of = og * 4 + of_l
            c_ev = []
            for rh in range(2):
                bank = G[rh]; bi = rh
                c_mm = None
                for dc in range(NDC):
                    w = []
                    if dc % 8 == 0:
                        w.append(("pool", c_cast[dc // 8]))
                    if dc == 0:
                        w.append(("act", c_xn_done))
                        if c_rope_prev is not None:
                            w.append(("dve", c_rope_prev))
                        if c_t2 is not None:
                            w.append(("pool", c_t2))
                    c_mm = mm(p, bank[:, :],
                              wbf[:, dc * 512 + of_l * 128:dc * 512 + of_l * 128 + 128],
                              xn[:, dc * 1024 + rh * 512:dc * 1024 + rh * 512 + 512],
                              dc == 0, dc == NDC - 1, w, inc="pe" if dc == NDC - 1 else None)
                c_ev.append(act_evac(p, fbuf[:, rh * 512:(rh + 1) * 512], bank[:, :],
                                     [("pe", c_mm)], bias=bq[:, of:of + 1]))
            c_xs = []
            for rh in range(2):
                w = [("act", c_ev[rh])]
                if c_t2 is not None:
                    w.append(("pool", c_t2))
                c_mm = mm(p, G[2 + rh][:, :], pswap_sb[:, :], fbuf[:, rh * 512:(rh + 1) * 512],
                          True, True, w, inc="pe")
                c_xs.append(act_evac(p, fbuf[:, 1024 + rh * 512:1024 + (rh + 1) * 512],
                                     G[2 + rh][:, :], [("pe", c_mm)]))
            c_t1 = dve_op(p, lambda e, o=fbuf[:, 0:1024], a=fbuf[:, 0:1024],
                          b=ropec_sb[:, :]: e.tensor_mul(o, a, b),
                          [("act", c_xs[1]), ("ld", c_ropetab)])
            c_t2 = dve_op(p, lambda e, o=fbuf[:, 1024:2048], a=fbuf[:, 1024:2048],
                          b=ropes_sb[:, :]: e.tensor_mul(o, a, b),
                          [("act", c_xs[1]), ("ld", c_ropetab)], eng="pool", inc="pool")
            s = stg[of % 2]
            w = [("dve", c_t1), ("pool", c_t2)]
            if n_store >= 2:
                w.append(("sta", (n_store - 1) * 16))
            c_rope_prev = dve_op(p, lambda e, o=s[:, :], a=fbuf[:, 0:1024],
                                 b=fbuf[:, 1024:2048]: e.tensor_add(o, a, b), w)
            dst = qT_h if of < 32 else kT_h
            p.dma("act", dst[(of % 32 if of < 32 else of - 32) * 128:
                             (of % 32 if of < 32 else of - 32) * 128 + 128, :],
                  s[:, :], [("dve", c_rope_prev)], inc="sta")
            n_store += 1
        c_war_pe = p.cnt["pe"]

    n_std = 1
    c_std_prev = {}
    for ogv in range(2):
        c_cast = wload(p, wqkv_h, NDC, 5120 + ogv * 512, 512, wst, wst2, wbf, c_war_pe, ret_all=True)
        for rs in range(8):
            bank = G[rs % 2]; bi = rs % 2
            c_mm = None
            for dc in range(NDC):
                w = []
                if dc % 8 == 0:
                    w.append(("pool", c_cast[dc // 8]))
                if dc == 0:
                    w.append(("act", c_vbb))
                    if bi in c_std_prev:
                        w.append(("dve", c_std_prev[bi]))
                c_mm = mm(p, bank[:, :], xn[:, dc * 1024 + rs * 128:dc * 1024 + rs * 128 + 128],
                          wbf[:, dc * 512:(dc + 1) * 512], dc == 0, dc == NDC - 1,
                          w, inc="pe" if dc == NDC - 1 else None)
            sv = stgv[rs % 2]
            w = [("pe", c_mm), ("sta", p.cnt["sta"])]
            if n_std >= 3:
                w.append(("std", (n_std - 1) * 16))
            c_ev = dve_op(p, lambda e, o=sv, a=bank[:, :], b=vbb[:, ogv * 512:(ogv + 1) * 512]:
                          e.tensor_add(o, a, b), w)
            c_std_prev[bi] = c_ev
            p.dma("act", v_h[rs * 128:(rs + 1) * 128, ogv * 512:(ogv + 1) * 512], sv,
                  [("dve", c_ev)], inc="std")
            n_std += 1
        c_war_pe = p.cnt["pe"]

    c_at = wload(p, at_h, NDC, 0, 128, wst, wst2, wbf, c_war_pe)
    c_h = None
    for rh in range(2):
        bank = G[3]
        c_mm = None
        for dc in range(NDC):
            w = [("pool", c_at)] if dc == 0 else []
            if dc == 0:
                w.append(("act", c_h if c_h is not None else p.cnt["act"]))
            c_mm = mm(p, bank[:, :], wbf[:, dc * 128:(dc + 1) * 128],
                      xn[:, dc * 1024 + rh * 512:dc * 1024 + rh * 512 + 512],
                      dc == 0, dc == NDC - 1, w, inc="pe" if dc == NDC - 1 else None)
        c_h = act_evac(p, stg[0][:, rh * 512:(rh + 1) * 512], bank[:, :],
                       [("pe", c_mm), ("sta", p.cnt["sta"]), ("std", p.cnt["std"])])
    p.dma("act", h1_h[:, :], stg[0][:, :], [], inc="sta")

    c_bcb = None
    for oc in range(8):
        c_cast = wload(p, bwt_h, NDC, oc * 512, 512, wst, wst2, wbf, p.cnt["pe"], ret_all=True)
        c_bb = p.dma("sync", bbrow[0:1, :], bb_h[oc * 512:(oc + 1) * 512],
                     [("act", c_bcb)] if c_bcb else [])
        c_bcb = bcast_row(p, G[2], bcb[:, :], bbrow[0:1, :],
                          onr_sb[:, :], 512,
                          [("ld", c_bb), ("dve", p.cnt["dve"]), ("act", p.cnt["act"])])
        for rs in range(8):
            bank = G[rs % 2]; bi = rs % 2
            c_mm = None
            for dc in range(NDC):
                w = []
                if dc % 8 == 0:
                    w.append(("pool", c_cast[dc // 8]))
                if dc == 0:
                    if bi in c_std_prev:
                        w.append(("dve", c_std_prev[bi]))
                c_mm = mm(p, bank[:, :], xn[:, dc * 1024 + rs * 128:dc * 1024 + rs * 128 + 128],
                          wbf[:, dc * 512:(dc + 1) * 512], dc == 0, dc == NDC - 1,
                          w, inc="pe" if dc == NDC - 1 else None)
            sf = stgf[rs % 2]
            w = [("pe", c_mm), ("act", c_bcb)]
            if n_std >= 3:
                w.append(("std", (n_std - 1) * 16))
            c_ev = dve_op(p, lambda e, o=sf[:, :], a=bank[:, :], b=bcb[:, :]:
                          e.tensor_add(o, a, b), w)
            c_std_prev[bi] = c_ev
            p.dma("act", base_h[rs * 128:(rs + 1) * 128, oc * 512:(oc + 1) * 512], sf[:, :],
                  [("dve", c_ev)], inc="std")
            n_std += 1

    p.wait_only("sync", [("sta", p.cnt["sta"]), ("std", p.cnt["std"])])
    p.finalize()
    st.close()
    return nc


def build_D():
    nc = bass.Bass("TRN2", target_bir_lowering=False)
    st = ExitStack()
    x1_h = nc.dram_tensor("x1", [R, D], F32, kind="ExternalInput").ap()
    qT_h = nc.dram_tensor("qT", [4096, R], BF16, kind="ExternalInput").ap()
    KT_h = nc.dram_tensor("KT", [1024, 4096], BF16, kind="ExternalInput").ap()
    V_h = nc.dram_tensor("V", [4096, 1024], BF16, kind="ExternalInput").ap()
    wo_h = nc.dram_tensor("wo", [D, D], F32, kind="ExternalInput").ap()
    bo_h = nc.dram_tensor("bo", [D], F32, kind="ExternalInput").ap()
    base_h = nc.dram_tensor("base", [R, D], F32, kind="ExternalInput").ap()
    h1_h = nc.dram_tensor("h1", [128, R], BF16, kind="ExternalInput").ap()
    qsums_h = nc.dram_tensor("qsums", [4, D], F32, kind="ExternalInput").ap()
    bfl_h = nc.dram_tensor("bfl", [128, D], F32, kind="ExternalInput").ap()
    rwt_h = nc.dram_tensor("rwt", [D, 8], F32, kind="ExternalInput").ap()
    rb_h = nc.dram_tensor("rb", [8], F32, kind="ExternalInput").ap()
    mask_h = nc.dram_tensor("mask", [4096, R], BF16, kind="ExternalInput").ap()
    idf_h = nc.dram_tensor("idf", [128, 128], F32, kind="ExternalInput").ap()
    onr_h = nc.dram_tensor("onr", [1, 128], F32, kind="ExternalInput").ap()
    onb_h = nc.dram_tensor("onb", [128, 1], BF16, kind="ExternalInput").ap()
    onc_h = nc.dram_tensor("onc", [128, 1], F32, kind="ExternalInput").ap()
    esel_h = nc.dram_tensor("esel", [8, 128], F32, kind="ExternalInput").ap()
    out_h = nc.dram_tensor("out", [R, D], F32, kind="ExternalOutput").ap()

    onc_sb = st.enter_context(nc.sbuf_tensor("oncs", [128, 1], F32))
    onr_sb = st.enter_context(nc.sbuf_tensor("onrs", [1, 128], F32))
    onb_sb = st.enter_context(nc.sbuf_tensor("onbs", [128, 1], BF16))
    idf_sb = st.enter_context(nc.sbuf_tensor("idfs", [128, 128], F32))
    esel_sb = st.enter_context(nc.sbuf_tensor("esels", [8, 128], F32))
    O_sb = st.enter_context(nc.sbuf_tensor("Osb", [128, 32 * 1024], BF16))
    hw1 = st.enter_context(nc.sbuf_tensor("hw1", [128, R], BF16))
    sc2 = st.enter_context(nc.sbuf_tensor("sc2", [1, 8], F32))
    negm = st.enter_context(nc.sbuf_tensor("negm", [128, 1], F32))
    recipb = st.enter_context(nc.sbuf_tensor("recipb", [128, 512], F32))
    rcp_row = st.enter_context(nc.sbuf_tensor("rcprow", [1, 512], F32))
    smal = st.enter_context(nc.sbuf_tensor("smal", [1, 1024], F32))
    rt_st = ExitStack()
    h1_sb = rt_st.enter_context(nc.sbuf_tensor("h1s", [128, R], BF16))
    qrow = rt_st.enter_context(nc.sbuf_tensor("qrow", [1, D], F32))
    qfm = rt_st.enter_context(nc.sbuf_tensor("qfm", [128, 32], F32))
    rwt_sb = rt_st.enter_context(nc.sbuf_tensor("rwts", [128, 8 * 32], F32))
    rb_sb = rt_st.enter_context(nc.sbuf_tensor("rbs", [8, 1], F32))
    lgT = rt_st.enter_context(nc.sbuf_tensor("lgT", [8, 1], F32))
    lg8 = rt_st.enter_context(nc.sbuf_tensor("lg8", [1, 8], F32))
    pr8 = rt_st.enter_context(nc.sbuf_tensor("pr8", [1, 8], F32))
    r8 = rt_st.enter_context(nc.sbuf_tensor("r8", [8, 1], F32))
    wex1 = rt_st.enter_context(nc.sbuf_tensor("wex1", [128, 1], F32))
    qs_sb = rt_st.enter_context(nc.sbuf_tensor("qss", [4, D], F32))

    G = [st.enter_context(nc.psum_tensor(f"G{i}", [128, 512], F32)) for i in range(4)]
    psV = st.enter_context(nc.psum_tensor("psV", [128, 512], F32))
    psTf = st.enter_context(nc.psum_tensor("psTf", [128, 128], F32))

    p = PB(nc, st)
    p.dma("sync", onr_sb[:, :], onr_h[:, :])
    p.dma("sync", onb_sb[:, :], onb_h[:, :])
    p.dma("sync", onc_sb[:, :], onc_h[:, :])
    p.dma("sync", idf_sb[:, :], idf_h[:, :])
    p.dma("sync", esel_sb[:, :], esel_h[:, :])
    p.dma("sync", rb_sb[:, 0:1], rb_h[:])
    p.dma("sync", h1_sb[:, :], h1_h[:, :])
    c_qs = p.dma("sync", qs_sb[:, :], qsums_h[:, :])
    c_rw = None
    for dc in range(NDC):
        c_rw = p.dma("sync", rwt_sb[:, dc * 8:(dc + 1) * 8], rwt_h[dc * 128:(dc + 1) * 128, :])

    c_mm = None
    c_ev = None
    for ch in range(8):
        w = [("ld", c_qs)] if ch == 0 else [("act", c_ev)]
        c_mm = mm(p, psV[0:1, :], onc_sb[0:4, 0:1],
                  qs_sb[:, ch * 512:(ch + 1) * 512], True, True, w,
                  inc="pe")
        c_ev = act_evac(p, qrow[0:1, ch * 512:(ch + 1) * 512], psV[0:1, :], [("pe", c_mm)])
    c_qrow = c_ev
    c_qf = None
    for c in range(32):
        c_qf = p.dma("sync", qfm[:, c:c + 1], qrow[0:1, c * 128:(c + 1) * 128],
                     [("act", c_qrow)])
    c_mm = None
    for c in range(32):
        c_mm = mm(p, G[0][0:8, 0:1], rwt_sb[:, c * 8:(c + 1) * 8], qfm[:, c:c + 1],
                  c == 0, c == 31,
                  [("ld", c_qf), ("ld", c_rw)] if c == 0 else [],
                  inc="pe" if c == 31 else None)
    c_lgT = act_evac(p, lgT[:, :], G[0][0:8, 0:1], [("pe", c_mm)],
                     bias=rb_sb[:, 0:1], scale=1.0 / 3484.0)
    c_tr = p.emit("pe", lambda e, o=psTf[0:1, 0:8], i=lgT[:, :], idn=idf_sb[0:8, 0:8]:
                  e.transpose(o, i, idn), [("act", c_lgT)], "pe")
    c_lg8 = act_evac(p, lg8[:, :], psTf[0:1, 0:8], [("pe", c_tr)])
    dve_op(p, lambda e, o=sc2[0:1, 0:1], i=lg8[0:1, :]: e.reduce_max(o, i, axis=AX),
           [("act", c_lg8)])
    c_nm = dve_op(p, lambda e, o=sc2[0:1, 1:2], i=sc2[0:1, 0:1]: e.tensor_scalar_mul(o, i, -1.0))
    c_eu = act_evac(p, pr8[:, :], lg8[:, :], [("dve", c_nm)], func=AF.Exp, bias=sc2[0:1, 1:2])
    dve_op(p, lambda e, o=sc2[0:1, 2:3], i=pr8[0:1, :]: e.reduce_sum(o, i, axis=AX),
           [("act", c_eu)])
    dve_op(p, lambda e, o=sc2[0:1, 3:4], i=sc2[0:1, 2:3]: e.reciprocal(o, i))
    c_pr = dve_op(p, lambda e, o=pr8[:, :], i=pr8[:, :], s=sc2[0:1, 3:4]:
                  e.tensor_scalar_mul(o, i, s))
    c_tr = p.emit("pe", lambda e, o=psTf[0:8, 0:1], i=pr8[0:1, :], idn=idf_sb[0:1, 0:1]:
                  e.transpose(o, i, idn), [("dve", c_pr)], "pe")
    c_r8 = act_evac(p, r8[:, :], psTf[0:8, 0:1], [("pe", c_tr)])
    c_mm = mm(p, G[0][:, 0:1], esel_sb[:, :], r8[:, 0:1], True, True, [("act", c_r8)], inc="pe")
    c_wex = act_evac(p, wex1[:, :], G[0][:, 0:1], [("pe", c_mm)])
    c_hw1 = dve_op(p, lambda e, o=hw1[:, :], i=h1_sb[:, :], s=wex1[:, :]:
                   e.tensor_scalar_mul(o, i, s), [("act", c_wex), ("ld", c_qs)])

    rt_st.close()
    attn_st = ExitStack()
    PT = attn_st.enter_context(nc.sbuf_tensor("PT", [128, 32 * 512], BF16))
    msk = attn_st.enter_context(nc.sbuf_tensor("msk", [128, 32 * 512], BF16))
    kh = [attn_st.enter_context(nc.sbuf_tensor(f"kh{i}", [128, 4096], BF16)) for i in range(2)]
    vh = [attn_st.enter_context(nc.sbuf_tensor(f"vh{i}", [128, 32 * 128], BF16)) for i in range(2)]
    qh = [attn_st.enter_context(nc.sbuf_tensor(f"qh{i}", [128, 512], BF16)) for i in range(2)]
    sqb = attn_st.enter_context(nc.sbuf_tensor("sqb", [128, 1024], BF16))
    qn = attn_st.enter_context(nc.sbuf_tensor("qn", [128, 1024], BF16))

    c_prev = None
    c_mq = c_mk = None
    for t in range(40):
        nch = 2 if t < 32 else 8
        if t < 32:
            buf = qn
            c_l = p.dma("sync", buf[:, :], qT_h[t * 128:(t + 1) * 128, :],
                        ([("act", c_prev)] if c_prev else []) +
                        ([("dve", c_hw1)] if t == 0 else []))
            c_prev = act_evac(p, sqb[:, :], buf[:, :],
                              [("ld", c_l), ("pe", p.cnt["pe"])], func=AF.Square)
            sqv = sqb
        else:
            buf = kh[0]
            c_l = p.dma("sync", buf[:, :], KT_h[(t - 32) * 128:(t - 32) * 128 + 128, :],
                        [("act", c_prev)] if c_prev else [])
            c_prev = act_evac(p, kh[1][:, :], buf[:, :],
                              [("ld", c_l), ("pe", p.cnt["pe"])], func=AF.Square)
            sqv = kh[1]
        for half in range(nch):
            w = [("act", c_prev)]
            if c_mq is not None:
                w.append(("dve", c_mq if t < 32 else (c_mk or c_mq)))
            c_mmn = mm(p, psV[0:1, :], onb_sb[:, :], sqv[:, half * 512:(half + 1) * 512],
                       True, True, w, inc="pe")
            c_red = dve_op(p, lambda e, o=sc2[0:1, 4:5], i=psV[0:1, :]:
                           e.reduce_max(o, i, axis=AX), [("pe", c_mmn)])
            idx = 5 if t < 32 else 6
            if (t == 0 and half == 0) or (t == 32 and half == 0):
                cc = dve_op(p, lambda e, o=sc2[0:1, idx:idx + 1], i=sc2[0:1, 4:5]:
                            e.tensor_copy(o, i))
            else:
                cc = dve_op(p, lambda e, o=sc2[0:1, idx:idx + 1], a=sc2[0:1, idx:idx + 1],
                            b=sc2[0:1, 4:5]: e.tensor_max(o, a, b))
            if t < 32:
                c_mq = cc
            else:
                c_mk = cc
    SCALE1 = 1.0 / float(np.sqrt(np.float64(128.0)))
    c_m2 = dve_op(p, lambda e, o=sc2[0:1, 0:1], a=sc2[0:1, 5:6], b=sc2[0:1, 6:7]:
                  e.tensor_mul(o, a, b), [("dve", c_mk)])
    c_sqr = act_evac(p, sc2[0:1, 1:2], sc2[0:1, 0:1], [("dve", c_m2)], func=AF.Sqrt)
    c_neg = dve_op(p, lambda e, o=sc2[0:1, 2:3], i=sc2[0:1, 1:2]:
                   e.tensor_scalar_mul(o, i, -SCALE1), [("act", c_sqr)])
    c_mmn = mm(p, G[3][:, 0:1], onr_sb[:, :], sc2[0:1, 2:3], True, True,
               [("dve", c_neg)], inc="pe")
    c_negb = act_evac(p, negm[:, :], G[3][:, 0:1], [("pe", c_mmn)])
    c_o_ev = None
    c_exp_prev = {}
    c_pv_done = None
    for qc in range(2):
        c_msk = None
        for kt in range(32):
            c_msk = p.dma("sync", msk[:, kt * 512:(kt + 1) * 512],
                          mask_h[kt * 128:(kt + 1) * 128, qc * 512:(qc + 1) * 512],
                          [("pool", p.cnt["pool"]), ("dve", p.cnt["dve"])] if kt == 0 and qc == 1 else [])
        for kvh in range(8):
            kb = kh[kvh % 2]
            vb = vh[kvh % 2]
            w_k = [("pe", c_pv_done)] if c_pv_done else []
            if qc == 0 and kvh == 0:
                w_k = [("pe", p.cnt["pe"]), ("act", c_negb)]
            c_kl = p.dma("sync", kb[:, :], KT_h[kvh * 128:(kvh + 1) * 128, :], w_k)
            c_vl = None
            for kt in range(32):
                c_vl = p.dma("sync", vb[:, kt * 128:(kt + 1) * 128],
                             V_h[kt * 128:(kt + 1) * 128, kvh * 128:(kvh + 1) * 128])
            for qx in range(4):
                hq = kvh * 4 + qx
                qb = qh[hq % 2]
                c_ql = p.dma("sync", qb[:, :], qT_h[hq * 128:(hq + 1) * 128, qc * 512:(qc + 1) * 512],
                             [("pe", c_pv_done)] if c_pv_done else [])
                c_exp = {}
                c_mask = {}
                for kt in range(32):
                    bank = G[kt % 3]
                    w = [("ld", c_ql), ("ld", c_kl)]
                    prev = c_exp_prev.get(kt % 3)
                    if prev:
                        w.append(("act", prev))
                    c_mm = mm(p, bank[:, :], kb[:, kt * 128:(kt + 1) * 128], qb[:, :],
                              True, True, w, inc="pe")
                    c_exp[kt] = act_evac(p, PT[:, kt * 512:(kt + 1) * 512], bank[:, :],
                                         [("pe", c_mm), ("act", c_negb), ("pe", c_pv_done)],
                                         func=AF.Exp, bias=negm[:, :], scale=SCALE1)
                    c_exp_prev[kt % 3] = c_exp[kt]
                    eng = "dve" if kt % 2 == 0 else "pool"
                    c_mask[kt] = dve_op(p, lambda e, o=PT[:, kt * 512:(kt + 1) * 512],
                                        a=PT[:, kt * 512:(kt + 1) * 512],
                                        b=msk[:, kt * 512:(kt + 1) * 512]:
                                        e.tensor_mul(o, a, b),
                                        [("act", c_exp[kt]), ("ld", c_msk)], eng=eng, inc=eng)
                c_mm = None
                for kt in range(32):
                    w = [("dve", c_mask[kt])] if kt % 2 == 0 else [("pool", c_mask[kt])]
                    if kt == 0 and c_o_ev is not None:
                        w.append(("dve", c_o_ev))
                    c_mm = mm(p, psV[0:1, :], onb_sb[:, :], PT[:, kt * 512:(kt + 1) * 512],
                              kt == 0, kt == 31, w, inc="pe" if kt == 31 else None)
                c_rcp = dve_op(p, lambda e, o=rcp_row[0:1, :], i=psV[0:1, :]:
                               e.reciprocal(o, i), [("pe", c_mm)])
                c_mm = mm(p, psV[:, :], onr_sb[:, :], rcp_row[0:1, :], True, True,
                          [("dve", c_rcp), ("act", c_negb)], inc="pe")
                w = [("pe", c_mm)]
                if c_o_ev is not None:
                    w.append(("dve", c_o_ev))
                c_rb2 = act_evac(p, recipb[:, :], psV[:, :], w)
                c_mm = None
                for kt in range(32):
                    w = [("ld", c_vl)]
                    if kt == 0 and c_o_ev is not None:
                        w.append(("dve", c_o_ev))
                    c_mm = mm(p, G[3][:, :], vb[:, kt * 128:(kt + 1) * 128],
                              PT[:, kt * 512:(kt + 1) * 512], kt == 0, kt == 31,
                              w if kt == 0 else [], inc="pe" if kt == 31 else None)
                c_pv_done = c_mm
                c_o_ev = dve_op(p, lambda e,
                                o=O_sb[:, hq * 1024 + qc * 512:hq * 1024 + qc * 512 + 512],
                                a=G[3][:, :], b=recipb[:, :]: e.tensor_mul(o, a, b),
                                [("pe", c_mm), ("act", c_rb2)])

    attn_pe = p.cnt["pe"]
    attn_dve = p.cnt["dve"]
    attn_st.close()

    wst = st.enter_context(nc.sbuf_tensor("wst", [128, 8 * 512], F32))
    wst2 = st.enter_context(nc.sbuf_tensor("wst2", [128, 8 * 512], F32))
    wbf = st.enter_context(nc.sbuf_tensor("wbf", [128, 32 * 512], BF16))
    bfst = st.enter_context(nc.sbuf_tensor("bfst", [128, D], F32))
    bfl_bf = st.enter_context(nc.sbuf_tensor("bflbf", [128, D], BF16))
    xcol = st.enter_context(nc.sbuf_tensor("xcol", [128, 8 * 512], F32))
    bcol = st.enter_context(nc.sbuf_tensor("bcol", [128, 8 * 512], F32))
    borow = st.enter_context(nc.sbuf_tensor("borow", [1, 512], F32))
    bob = st.enter_context(nc.sbuf_tensor("bob", [128, 512], F32))
    stgf = [st.enter_context(nc.sbuf_tensor(f"stgf{i}", [128, 512], F32)) for i in range(2)]

    p.wait_only("sync", [("dve", attn_dve)])
    c_bf = None
    for i in range(4):
        c_bf = p.dma("sync", bfst[:, i * 1024:(i + 1) * 1024], bfl_h[:, i * 1024:(i + 1) * 1024])
    c_bfl = dve_op(p, lambda e, o=bfl_bf[:, :], i=bfst[:, :]: e.tensor_copy(o, i),
                   [("ld", c_bf)], eng="pool", inc="pool")

    n_std = 0
    c_res_prev = None
    c_std_prev = {}
    c_bob = None
    for oc in range(8):
        c_cast = wload(p, wo_h, NDC, oc * 512, 512, wst, wst2, wbf,
                       attn_pe if oc == 0 else p.cnt["pe"], ret_all=True)
        c_xc = bc = None
        for rs in range(8):
            w = [("dve", c_res_prev)] if (oc > 0 and rs == 0 and c_res_prev) else []
            c_xc = p.dma("sync", xcol[:, rs * 512:(rs + 1) * 512],
                         x1_h[rs * 128:(rs + 1) * 128, oc * 512:(oc + 1) * 512], w)
        for rs in range(8):
            bc = p.dma("sync", bcol[:, rs * 512:(rs + 1) * 512],
                       base_h[rs * 128:(rs + 1) * 128, oc * 512:(oc + 1) * 512])
        c_bo = p.dma("sync", borow[0:1, :], bo_h[oc * 512:(oc + 1) * 512])
        c_bob = bcast_row(p, G[3], bob[:, :], borow[0:1, :], onr_sb[:, :], 512,
                          [("ld", c_bo), ("dve", p.cnt["dve"])] +
                          ([("act", c_bob)] if c_bob else []))
        for rs in range(8):
            qc = rs // 4
            bank = G[rs % 2]; bi = rs % 2
            c_mm = None
            for dc in range(NDC):
                w = []
                if dc % 8 == 0:
                    w.append(("pool", c_cast[dc // 8]))
                if dc == 0:
                    w += [("dve", c_hw1), ("dve", attn_dve)]
                    if bi in c_std_prev:
                        w.append(("dve", c_std_prev[bi]))
                c_mm = mm(p, bank[:, :],
                          O_sb[:, dc * 1024 + qc * 512 + (rs % 4) * 128:
                               dc * 1024 + qc * 512 + (rs % 4) * 128 + 128],
                          wbf[:, dc * 512:(dc + 1) * 512], dc == 0, False, w)
            c_mm = mm(p, bank[:, :], hw1[:, rs * 128:(rs + 1) * 128],
                      bfl_bf[:, oc * 512:(oc + 1) * 512], False, True,
                      [("pool", c_bfl)], inc="pe")
            sf = stgf[rs % 2]
            w = [("pe", c_mm), ("act", c_bob), ("ld", c_xc), ("ld", bc)]
            if n_std >= 2:
                w.append(("std", (n_std - 1) * 16))
            dve_op(p, lambda e, o=sf[:, :], a=bank[:, :], b=bob[:, :]: e.tensor_add(o, a, b), w)
            dve_op(p, lambda e, o=sf[:, :], a=sf[:, :], b=xcol[:, rs * 512:(rs + 1) * 512]:
                   e.tensor_add(o, a, b))
            c_ev = dve_op(p, lambda e, o=sf[:, :], a=sf[:, :], b=bcol[:, rs * 512:(rs + 1) * 512]:
                          e.tensor_add(o, a, b))
            c_std_prev[bi] = c_ev
            c_res_prev = c_ev
            p.dma("act", out_h[rs * 128:(rs + 1) * 128, oc * 512:(oc + 1) * 512], sf[:, :],
                  [("dve", c_ev)], inc="std")
            n_std += 1

    p.wait_only("sync", [("std", p.cnt["std"])])
    p.finalize()
    st.close()
    return nc


def _consts():
    idb = np.eye(128, dtype=BF)
    idf = np.eye(128, dtype=np.float32)
    onr = np.ones((1, 128), dtype=np.float32)
    onb = np.ones((128, 1), dtype=BF)
    psw = np.zeros((128, 128), dtype=np.float32)
    for i in range(128):
        psw[i, i ^ 1] = 1.0
    esel = np.zeros((8, 128), dtype=np.float32)
    for m in range(128):
        esel[m // 16, m] = 2.0
    inv = (1.0 / (np.float32(10000.0) ** (np.arange(0, 128, 2, dtype=np.float32)
                                          / np.float32(128.0)))).astype(np.float32)
    pos = np.arange(4096, dtype=np.float32)
    ang = inv[:, None] * pos[None, :]
    cosv = np.cos(ang).astype(np.float32)
    sinv = np.sin(ang).astype(np.float32)
    ropec = np.empty((128, 4096), dtype=np.float32)
    ropes = np.empty((128, 4096), dtype=np.float32)
    ropec[0::2, :] = cosv
    ropec[1::2, :] = cosv
    ropes[0::2, :] = -sinv
    ropes[1::2, :] = sinv
    return idb, idf, onr, onb, psw, esel, ropec, ropes


def kernel(**inp):
    f32 = np.float32
    idb, idf, onr, onb, psw, esel, ropec, ropes = _consts()

    x = np.ascontiguousarray(inp["L0_attn_x"], dtype=f32)
    wqkv0 = np.ascontiguousarray(inp["L0_attn_Wqkv"], dtype=f32)
    bqkv0 = np.ascontiguousarray(inp["L0_attn_bqkv"], dtype=f32)
    wo0 = np.ascontiguousarray(inp["L0_attn_Wo"], dtype=f32)
    bo0 = np.ascontiguousarray(inp["L0_attn_bo"], dtype=f32)
    bwt0 = np.ascontiguousarray(inp["L0_ffn_base_W"].T, dtype=f32)
    bb0 = np.ascontiguousarray(inp["L0_ffn_base_b"], dtype=f32)
    rwt0 = np.ascontiguousarray(inp["L0_ffn_router_W"].T, dtype=f32)
    rb0 = np.ascontiguousarray(inp["L0_ffn_router_b"], dtype=f32)
    at0 = np.ascontiguousarray(inp["L0_ffn_A"].reshape(128, 4096).T, dtype=f32)
    bfl0 = np.ascontiguousarray(inp["L0_ffn_B"].transpose(0, 2, 1).reshape(128, 4096), dtype=f32)
    em0 = np.ascontiguousarray(inp["L0_ffn_expert_map"], dtype=np.int32)
    w1 = np.ascontiguousarray(inp["L1_attn_w_qkv"].T, dtype=f32)
    b1 = np.ascontiguousarray(inp["L1_attn_b_qkv"], dtype=f32)
    wo1 = np.ascontiguousarray(inp["L1_attn_w_o"].T, dtype=f32)
    bo1 = np.ascontiguousarray(inp["L1_attn_b_o"], dtype=f32)
    bwt1 = np.ascontiguousarray(inp["L1_ffn_W"].T, dtype=f32)
    bb1 = np.ascontiguousarray(inp["L1_ffn_b"], dtype=f32)
    at1 = np.ascontiguousarray(inp["L1_ffn_A"].reshape(128, 4096).T, dtype=f32)
    bfl1 = np.ascontiguousarray(inp["L1_ffn_B"].transpose(0, 2, 1).reshape(128, 4096), dtype=f32)
    rwt1 = np.ascontiguousarray(inp["L1_ffn_router_W"].T, dtype=f32)
    rb1 = np.ascontiguousarray(inp["L1_ffn_router_b"], dtype=f32)

    cores = list(range(NCORE))
    rows = [(c // 4, c % 4) for c in cores]

    ncA = build_A()
    in_maps = []
    for (b, j) in rows:
        in_maps.append(dict(
            x=np.ascontiguousarray(x[b, j * R:(j + 1) * R, :]),
            wqkv=wqkv0, bqkv=bqkv0, bwt=bwt0, bb=bb0, rwt=rwt0, rb=rb0,
            at=at0, bfl=bfl0, em=em0, idb=idb, idf=idf, onr=onr, esel=esel))
    resA = run_bass_kernel_spmd(ncA, in_maps, core_ids=cores)
    qT0 = [resA.results[c]["qT"] for c in cores]
    kT0 = [resA.results[c]["kT"] for c in cores]
    v0 = [resA.results[c]["v"] for c in cores]
    ffn0 = [resA.results[c]["ffn"] for c in cores]

    KT0 = []
    V0 = []
    for b in range(2):
        K = np.concatenate([kT0[4 * b + j] for j in range(4)], axis=1)
        V = np.concatenate([v0[4 * b + j] for j in range(4)], axis=0)
        KT0.append(np.ascontiguousarray(K))
        V0.append(np.ascontiguousarray(V))

    ncB = build_B()
    in_maps = []
    for c, (b, j) in enumerate(rows):
        in_maps.append(dict(
            x=np.ascontiguousarray(x[b, j * R:(j + 1) * R, :]),
            qT=qT0[c], KT=KT0[b], V=V0[b], wo=wo0, bo=bo0, ffn=ffn0[c],
            onr=onr, onb=onb))
    resB = run_bass_kernel_spmd(ncB, in_maps, core_ids=cores)
    x1 = [resB.results[c]["x1"] for c in cores]

    ncC = build_C()
    in_maps = []
    for c, (b, j) in enumerate(rows):
        mcol = np.zeros((R,), dtype=BF)
        for r_ in range(R):
            g = j * R + r_
            if 611 <= g <= 4094:
                mcol[r_] = 1.0
        in_maps.append(dict(
            x=x1[c], wqkv=w1, bqkv=b1, bwt=bwt1, bb=bb1, at=at1,
            ropec=np.ascontiguousarray(ropec[:, j * R:(j + 1) * R]),
            ropes=np.ascontiguousarray(ropes[:, j * R:(j + 1) * R]),
            pswap=psw, mcol=mcol, idb=idb, onr=onr))
    resC = run_bass_kernel_spmd(ncC, in_maps, core_ids=cores)
    qT1 = [resC.results[c]["qT"] for c in cores]
    kT1 = [resC.results[c]["kT"] for c in cores]
    v1 = [resC.results[c]["v"] for c in cores]
    base1 = [resC.results[c]["base"] for c in cores]
    h1 = [resC.results[c]["h1"] for c in cores]
    qsum1 = [resC.results[c]["qsum"] for c in cores]

    KT1 = []
    V1 = []
    QS = []
    for b in range(2):
        KT1.append(np.ascontiguousarray(np.concatenate([kT1[4 * b + j] for j in range(4)], axis=1)))
        V1.append(np.ascontiguousarray(np.concatenate([v1[4 * b + j] for j in range(4)], axis=0)))
        QS.append(np.ascontiguousarray(np.concatenate([qsum1[4 * b + j] for j in range(4)], axis=0)))

    ncD = build_D()
    in_maps = []
    for c, (b, j) in enumerate(rows):
        maskc = np.zeros((4096, R), dtype=BF)
        kk = np.arange(4096)[:, None]
        qq = j * R + np.arange(R)[None, :]
        maskc[:] = (kk <= qq).astype(BF)
        in_maps.append(dict(
            x1=x1[c], qT=qT1[c], KT=KT1[b], V=V1[b], wo=wo1, bo=bo1,
            base=base1[c], h1=h1[c], qsums=QS[b], bfl=bfl1, rwt=rwt1, rb=rb1,
            mask=maskc, idf=idf, onr=onr, onb=onb,
            onc=np.ones((128, 1), dtype=np.float32), esel=esel))
    resD = run_bass_kernel_spmd(ncD, in_maps, core_ids=cores)
    out = np.empty((2, 4096, 4096), dtype=f32)
    for c, (b, j) in enumerate(rows):
        out[b, j * R:(j + 1) * R, :] = resD.results[c]["out"]
    return out
